# version 09
# baseline (speedup 1.0000x reference)
import contextlib

import numpy as np
import concourse.bass as bass
import concourse.mybir as mybir
from concourse.bass_utils import run_bass_kernel_spmd

F32 = mybir.dt.float32
BF16 = mybir.dt.bfloat16
AF = mybir.ActivationFunctionType
ALU = mybir.AluOpType

D = 2048
LI = 2048
ITEMS = 2
NTOK = LI * ITEMS
NTI = LI // 128
NTT = NTOK // 128
HF = 8192
NH = 16
HD = 128
BLK = 256
NB = LI // BLK
CH = 768
SHIFT = 12.0
QK_SCALE = float(1.0 / np.sqrt(np.float32(128.0)))
EPS_LN = 1e-5
EPS_RMS = 1e-6
f32 = np.float32
SCAN_NH = NH
SCAN_NB = NB


class Sem:
    def __init__(self, ctx, nc, name):
        self.h = ctx.enter_context(nc.semaphore(name))
        self.n = 0

    def inc(self, instr, k=1):
        instr.then_inc(self.h, k)
        self.n += k
        return self.n


def build(debug_names=(), stop_after=None):
    nc = bass.Bass("TRN2", target_bir_lowering=False, debug=False)
    ctx = contextlib.ExitStack()

    def dram(name, shape, dtype, kind):
        if kind == "Internal" and name in debug_names:
            kind = "ExternalOutput"
        return nc.dram_tensor(name, shape, dtype, kind=kind).ap()

    x_in = dram("x", [NTOK, D], F32, "ExternalInput")
    wqkv = dram("wqkv", [D, 3 * D], F32, "ExternalInput")
    bqkv_cols = dram("bqkv_cols", [128, 48], F32, "ExternalInput")
    bout_row = dram("bout_row", [1, D], F32, "ExternalInput")
    wouta = dram("wouta", [D, D], F32, "ExternalInput")
    wqkvu = dram("wqkvu", [HF, D], F32, "ExternalInput")
    woutb = dram("woutb", [D, D], F32, "ExternalInput")
    slope = dram("slope", [128, NH], F32, "ExternalInput")
    wg = dram("wg", [HF, D], F32, "ExternalInput")
    wu = dram("wu", [HF, D], F32, "ExternalInput")
    wd = dram("wd", [D, HF], F32, "ExternalInput")
    iota_row = dram("iota_row", [128, BLK], F32, "ExternalInput")
    iota_col2 = dram("iota_col2", [128, 2], F32, "ExternalInput")
    idxt = dram("idxt", [BLK, BLK], F32, "ExternalInput")
    maskt = dram("maskt", [BLK, BLK], F32, "ExternalInput")
    y = dram("y", [NTOK, D], F32, "ExternalOutput")

    xs1 = dram("xs1", [NTOK, D], F32, "Internal")
    xs2 = dram("xs2", [NTOK, D], F32, "Internal")
    xs3 = dram("xs3", [NTOK, D], F32, "Internal")
    xnscr = dram("xnscr", [NTOK, D], BF16, "Internal")
    wqkv_bf = dram("wqkv_bf", [D, 3 * D], BF16, "Internal")
    wouta_bf = dram("wouta_bf", [D, D], BF16, "Internal")
    woutb_bf = dram("woutb_bf", [D, D], BF16, "Internal")
    wqkvu_bf = dram("wqkvu_bf", [HF, D], BF16, "Internal")
    wg_bf = dram("wg_bf", [HF, D], BF16, "Internal")
    wu_bf = dram("wu_bf", [HF, D], BF16, "Internal")
    wd_bf = dram("wd_bf", [D, HF], BF16, "Internal")
    wqkvuT = dram("wqkvuT", [64, 16, 128, 128], BF16, "Internal")
    wguT = dram("wguT", [64, 2, 16, 128, 128], BF16, "Internal")
    wdT = dram("wdT", [HF, D], BF16, "Internal")
    qkvT = dram("qkvT", [3 * D, LI], BF16, "Internal")
    vU = dram("vU", [LI, D], BF16, "Internal")
    attn = dram("attn", [LI, D], BF16, "Internal")
    qkvuTb = dram("qkvuTb", [4 * D, LI], BF16, "Internal")
    kUb = dram("kUb", [LI, D], BF16, "Internal")
    vUb = dram("vUb", [LI, D], BF16, "Internal")

    BIGC = 65536
    big = nc.alloc_sbuf_tensor("big", [128, BIGC], BF16)
    SMC = 14336
    small = nc.alloc_sbuf_tensor("small", [128, SMC], BF16)
    xf32 = nc.alloc_sbuf_tensor("xf32", [128, 2 * D], F32)
    evf32 = nc.alloc_sbuf_tensor("evf32", [128, D], F32)
    xnorm = nc.alloc_sbuf_tensor("xnorm", [128, D], BF16)
    bn24 = nc.alloc_sbuf_tensor("bn24", [128, 24], F32)
    mvb = nc.alloc_sbuf_tensor("mvb", [128, 4], F32)
    lnvb = nc.alloc_sbuf_tensor("lnvb", [128, 2], F32)
    rstdb = nc.alloc_sbuf_tensor("rstdb", [128, 2], F32)
    bcols = nc.alloc_sbuf_tensor("bcols", [128, 48], F32)
    brow_bf = nc.alloc_sbuf_tensor("brow_bf", [1, D], BF16)
    ones_bf = nc.alloc_sbuf_tensor("ones_bf", [1, 128], BF16)
    ones_col = nc.alloc_sbuf_tensor("ones_col", [128, 1], BF16)
    slope_sb = nc.alloc_sbuf_tensor("slope_sb", [128, NH], F32)
    iota_sb = nc.alloc_sbuf_tensor("iota_sb", [128, BLK], F32)
    iotc_sb = nc.alloc_sbuf_tensor("iotc_sb", [128, 2], F32)
    idxt_sb = nc.alloc_sbuf_tensor("idxt_sb", [128, 2, BLK], BF16)
    maskt_sb = nc.alloc_sbuf_tensor("maskt_sb", [128, 2, BLK], BF16)
    rstdc = nc.alloc_sbuf_tensor("rstdc", [128, NTI], F32)
    lnc = nc.alloc_sbuf_tensor("lnc", [128, NTI], F32)
    kv_sb = nc.alloc_sbuf_tensor("kv_sb", [128, 128], F32)
    kv_bf = nc.alloc_sbuf_tensor("kv_bf", [128, 128], BF16)
    sq_bf = nc.alloc_sbuf_tensor("sq_bf", [128, BLK], BF16)
    o_sb = nc.alloc_sbuf_tensor("o_sb", [128, BLK], BF16)
    dec_f = nc.alloc_sbuf_tensor("dec_f", [128, BLK], F32)
    qdec = nc.alloc_sbuf_tensor("qdec", [128, BLK], BF16)
    ddm = nc.alloc_sbuf_tensor("ddm", [128, 2, BLK], BF16)
    dtmp = nc.alloc_sbuf_tensor("dtmp", [128, BLK], BF16)
    kdec = nc.alloc_sbuf_tensor("kdec", [128, 2], F32)
    bdcol = nc.alloc_sbuf_tensor("bdcol", [128, 1], F32)
    ksc = nc.alloc_sbuf_tensor("ksc", [128, 2, 128], BF16)
    qsc = nc.alloc_sbuf_tensor("qsc", [128, BLK], BF16)
    asb = nc.alloc_sbuf_tensor("asb", [128, 2, BLK], BF16)
    recb = nc.alloc_sbuf_tensor("recb", [128, 2], F32)

    ps = ctx.enter_context(nc.psum_tensor("ps", [128, 4096], F32))

    k = K = type("KK", (), {})()
    k.sW = Sem(ctx, nc, "sW")
    k.sA = Sem(ctx, nc, "sA")
    k.sB = Sem(ctx, nc, "sB")
    k.te = Sem(ctx, nc, "te")
    k.ve = Sem(ctx, nc, "ve")
    k.se = Sem(ctx, nc, "se")

    def dW(out, in_, **kw):
        return k.sW.inc(nc.gpsimd.dma_start(out=out, in_=in_, **kw), 16)

    def dA(out, in_, **kw):
        return k.sA.inc(nc.sync.dma_start(out=out, in_=in_, **kw), 16)

    def dB(out, in_, **kw):
        return k.sB.inc(nc.scalar.dma_start(out=out, in_=in_, **kw), 16)

    nct = nc.tensor
    ncv = nc.vector
    ncs = nc.scalar

    def barrier():
        for eng in (nct, ncv, ncs, nc.gpsimd, nc.sync):
            for s in (k.sW, k.sA, k.sB, k.te, k.ve, k.se):
                eng.wait_ge(s.h, s.n)

    xnT = big[:, 0:32768].rearrange("p (ds t) -> p ds t", t=2048)
    pts = big[:, 32768:40960].rearrange("p (kt q) -> p kt q", q=512)
    stage0 = big[:, 40960:49152]
    stage1 = big[:, 49152:57344]
    wouta_sb = big[:, 0:32768].rearrange("p (ds e) -> p ds e", e=2048)
    aoT = big[:, 32768:65536].rearrange("p (ds t) -> p ds t", t=2048)
    gTf = big[:, 0:49152].rearrange("p (h t) -> p h t", t=CH)
    xnT_c = big[:, 49152:61440].rearrange("p (ds t) -> p ds t", t=CH)
    gTb = big[:, 0:32768].rearrange("p (h t) -> p h t", t=2048)
    woutbT = big[:, 32768:65536].rearrange("p (h e) -> p h e", e=2048)

    wt0 = small[:, 0:2048]
    wt1 = small[:, 2048:4096]
    evst = small[:, 4096:6144]
    tst0 = small[:, 10240:12288]
    tst1 = small[:, 12288:14336]
    kth = small[:, 0:2048]
    qth = small[:, 2048:4096]
    vp = small[:, 4096:6208].rearrange("p (kt e) -> p kt e", e=132)
    oh = small[:, 6400:8448].rearrange("p (j e) -> p j e", e=128)
    wgu = [small[:, 0:4096].rearrange("p (g ds e) -> p g ds e", g=2, e=128),
           small[:, 4096:8192].rearrange("p (g ds e) -> p g ds e", g=2, e=128)]
    silu_sb = small[:, 8192:9088]
    wdt = [small[:, 9216:11264].rearrange("p (j e) -> p j e", e=512),
           small[:, 11264:13312].rearrange("p (j e) -> p j e", e=512)]
    qtb = small[:, 0:2048]
    ktb = small[:, 2048:4096]
    utb = small[:, 4096:6144]
    ks3 = small[:, 6144:8192].rearrange("p (j e) -> p j e", e=128)
    vs3 = small[:, 8192:10240].rearrange("p (j e) -> p j e", e=128)

    with nc.Block() as block:

        @block.sync
        def _(sync):
            dW(bcols[:, :], bqkv_cols[:, :])
            dW(brow_bf[0:1, :], bout_row[0:1, :])
            dW(slope_sb[:, :], slope[:, :])
            dW(iota_sb[:, :], iota_row[:, :])
            dW(iotc_sb[:, :], iota_col2[:, :])
            dW(idxt_sb[:, :, :], idxt.rearrange("(st p) m -> p st m", p=128))
            dW(maskt_sb[:, :, :], maskt.rearrange("(st p) m -> p st m", p=128))
            t_cst = k.sW.n
            ncv.wait_ge(k.sW.h, t_cst)
            ncv.memset(ones_bf[:, :], 1.0)
            ncv.memset(ones_col[:, :], 1.0)
            ncv.memset(ones_col[:, :], 1.0)

            def cast_w(dst, src, rows, blk_rows):
                for r0 in range(0, rows, blk_rows):
                    dW(dst[r0:r0 + blk_rows, :], src[r0:r0 + blk_rows, :])
                return k.sW.n

            t_wqkv_cast = cast_w(wqkv_bf, wqkv, D, 256)
            t_wouta_cast = cast_w(wouta_bf, wouta, D, 512)
            t_wg_cast = cast_w(wg_bf, wg, HF, 512)
            t_wu_cast = cast_w(wu_bf, wu, HF, 512)
            t_wd_cast = cast_w(wd_bf, wd, D, 128)
            t_wqkvu_cast = cast_w(wqkvu_bf, wqkvu, HF, 512)
            t_woutb_cast = cast_w(woutb_bf, woutb, D, 512)

            store_t = {}
            pp_i = [0]

            def pp_step(cast_target, emit_tr, emit_st):
                i = pp_i[0]
                st = (stage0, stage1)[i % 2]
                if i >= 2:
                    nc.scalar.wait_ge(k.sA.h, store_t[i - 2])
                nc.scalar.wait_ge(k.sW.h, cast_target)
                emit_tr(st)
                sync.wait_ge(k.sB.h, k.sB.n)
                emit_st(st)
                store_t[i] = k.sA.n
                pp_i[0] = i + 1

            def transpose_ffn_w(src_bf, dstT, cast_target):
                for ds in range(16):
                    pp_step(
                        cast_target,
                        lambda st, ds=ds: dB(st[:, :],
                                             src_bf[:, ds * 128:(ds + 1) * 128],
                                             transpose=True),
                        lambda st, ds=ds: dA(
                            dstT[:, ds].rearrange("ft p e -> p ft e"),
                            st.rearrange("p (ft e) -> p ft e", e=128)))

            transpose_ffn_w(wg_bf, wguT[:, 0], t_wg_cast)
            transpose_ffn_w(wu_bf, wguT[:, 1], t_wu_cast)
            transpose_ffn_w(wqkvu_bf, wqkvuT, t_wqkvu_cast)
            for hs in range(64):
                pp_step(
                    t_wd_cast,
                    lambda st, hs=hs: dB(st[:, 0:2048],
                                         wd_bf[:, hs * 128:(hs + 1) * 128],
                                         transpose=True),
                    lambda st, hs=hs: dA(wdT[hs * 128:(hs + 1) * 128, :],
                                         st[:, 0:2048]))

            barrier()

            def ln_core(src_sb, dst_r0, par, eps):
                xnt = xnorm[:, :]
                mv = mvb[:, par * 2:par * 2 + 2]
                lnv = lnvb[:, par:par + 1]
                rst = rstdb[:, par:par + 1]
                ncv.wait_ge(k.ve.h, k.ve.n)
                for i in range(4):
                    ins = ncv.bn_stats(bn24[:, 6 * i:6 * i + 6],
                                       src_sb[:, 512 * i:512 * (i + 1)])
                k.ve.inc(ins)
                ncv.wait_ge(k.ve.h, k.ve.n)
                t_mv = k.ve.inc(ncv.bn_aggr(mv, bn24[:, :]))
                ncs.wait_ge(k.ve.h, t_mv)
                ncs.wait_ge(k.se.h, k.se.n)
                t_ln = k.se.inc(ncs.activation(lnv, mv[:, 1:2], AF.Ln, bias=f32(eps)))
                ncs.wait_ge(k.se.h, t_ln)
                t_rs = k.se.inc(ncs.activation(rst, lnv, AF.Exp, scale=f32(-0.5)))
                ncv.wait_ge(k.se.h, t_rs)
                t_xn = k.ve.inc(ncv.tensor_scalar(xnt, src_sb, mv[:, 0:1], rst,
                                                  ALU.subtract, ALU.mult))
                sync.wait_ge(k.ve.h, t_xn)
                t_st = dA(xnscr[dst_r0:dst_r0 + 128, :], xnt)
                return t_st, t_xn

            xn_store = {0: 0}

            def ln_range(src, tok0, ntiles, eps=EPS_LN, x_ready=0):
                nc.scalar.wait_ge(k.sA.h, x_ready)
                prev = {}
                for tt in range(ntiles):
                    par = tt % 2
                    r0 = tok0 + tt * 128
                    xt = xf32[:, par * D:(par + 1) * D]
                    if tt >= 2:
                        nc.scalar.wait_ge(k.ve.h, prev[tt - 2][1])
                    ncv.wait_ge(k.sA.h, xn_store[0])
                    tld = dB(xt, src[r0:r0 + 128, :])
                    ncv.wait_ge(k.sB.h, tld)
                    prev[tt] = ln_core(xt, r0, par, eps)
                    xn_store[0] = prev[tt][0]
                return k.sA.n

            def load_xnT(dst3, tok0, ntok, xn_ready_sA):
                for ds in range(16):
                    nc.scalar.wait_ge(k.sA.h, xn_ready_sA)
                    dB(dst3[:, ds, 0:ntok],
                       xnscr[tok0:tok0 + ntok, ds * 128:(ds + 1) * 128],
                       transpose=True)
                return k.sB.n

            def residual_evac_tile(pY, xsrc, xdst, r0, par, rstd_col,
                                   state, do_ln):
                xt = xf32[:, par * D:(par + 1) * D]
                t_x = dB(xt, xsrc[r0:r0 + 128, :])
                ncv.wait_ge(k.sB.h, t_x)
                ncv.wait_ge(k.sA.h, state.get("ev_store", 0))
                ncv.wait_ge(k.ve.h, k.ve.n)
                if rstd_col is None:
                    t_ev = k.ve.inc(ncv.tensor_tensor(evf32[:, :], pY, xt, ALU.add))
                else:
                    t_ev = k.ve.inc(ncv.scalar_tensor_tensor(
                        evf32[:, :], pY, rstd_col, xt, ALU.mult, ALU.add))
                sync.wait_ge(k.ve.h, t_ev)
                dA(xdst[r0:r0 + 128, :], evf32[:, :])
                state["ev_store"] = k.sA.n
                if do_ln:
                    ncv.wait_ge(k.sA.h, xn_store[0])
                    t_st, _ = ln_core(evf32[:, :], r0, par, EPS_LN)
                    xn_store[0] = t_st
                return t_ev

            def attn_a(item, xsrc, xdst, xn_ready):
                tok0 = item * LI
                t_xnT = load_xnT(xnT, tok0, LI, xn_ready)
                nct.wait_ge(k.sB.h, t_xnT)

                wt_dma = {}
                mm_done = {}
                ev_done = {}
                for ft in range(48):
                    par = ft % 2
                    wt = (wt0, wt1)[par]
                    if ft >= 2:
                        nc.gpsimd.wait_ge(k.te.h, mm_done[ft - 2])
                    wt_dma[ft] = dW(
                        wt.rearrange("p (ds e) -> p ds e", e=128),
                        wqkv_bf[:, ft * 128:(ft + 1) * 128].rearrange(
                            "(ds p) e -> p ds e", p=128))
                    nct.wait_ge(k.sW.h, wt_dma[ft])
                    for qc in range(4):
                        gi = ft * 4 + qc
                        pb = ps[:, (gi % 2) * 512:(gi % 2) * 512 + 512]
                        if gi >= 2:
                            nct.wait_ge(k.ve.h, ev_done[gi - 2])
                        for ds in range(16):
                            ins = nct.matmul(pb, wt[:, ds * 128:(ds + 1) * 128],
                                             xnT[:, ds, qc * 512:(qc + 1) * 512],
                                             start=(ds == 0), stop=(ds == 15))
                        t_mm = k.te.inc(ins)
                        if qc == 3:
                            mm_done[ft] = t_mm
                        ncv.wait_ge(k.te.h, t_mm)
                        sc2 = QK_SCALE if ft < 16 else 1.0
                        ev_done[gi] = k.ve.inc(ncv.tensor_scalar(
                            evst[:, qc * 512:(qc + 1) * 512], pb,
                            bcols[:, ft:ft + 1], sc2, ALU.add, ALU.mult))
                    sync.wait_ge(k.ve.h, ev_done[ft * 4 + 3])
                    t_row = dA(qkvT[ft * 128:(ft + 1) * 128, :], evst[:, :])
                    ncv.wait_ge(k.sA.h, t_row)
                t_qkvT = k.sA.n

                tst_store = {}
                for ts in range(16):
                    st = (tst0, tst1)[ts % 2]
                    if ts >= 2:
                        nc.scalar.wait_ge(k.sA.h, tst_store[ts - 2])
                    nc.scalar.wait_ge(k.sA.h, t_qkvT)
                    dB(st[:, :], qkvT[2 * D:3 * D, ts * 128:(ts + 1) * 128],
                       transpose=True)
                    sync.wait_ge(k.sB.h, k.sB.n)
                    dA(vU[ts * 128:(ts + 1) * 128, :], st[:, :])
                    tst_store[ts] = k.sA.n
                t_vU = k.sA.n

                ncv.wait_ge(k.sA.h, t_vU)
                t_ones = k.ve.inc(ncv.memset(vp[:, :, 128:132], 1.0))
                ncv.wait_ge(k.ve.h, t_ones)
                oh_store = 0
                for h in range(NH):
                    sync.wait_ge(k.te.h, k.te.n)
                    sync.wait_ge(k.ve.h, k.ve.n)
                    sync.wait_ge(k.sA.h, t_vU)
                    dA(kth[:, :], qkvT[D + h * 128:D + (h + 1) * 128, :])
                    dA(qth[:, :], qkvT[h * 128:(h + 1) * 128, :])
                    dA(vp[:, :, 0:128],
                       vU[:, h * 128:(h + 1) * 128].rearrange("(kt p) e -> p kt e", p=128))
                    t_ld = k.sA.n
                    nct.wait_ge(k.sA.h, t_ld)
                    exp_done = {}
                    pv_ev = {}
                    pts_flat = big[:, 32768:40960]
                    for qc in range(4):
                        t_exp_last = 0
                        for g in range(4):
                            gi = qc * 4 + g
                            par = gi % 2
                            base = par * 2048
                            if gi >= 2:
                                nct.wait_ge(k.se.h, exp_done[gi - 2])
                            if par == 1 and 15 in pv_ev and qc > 0 and g == 1:
                                pass
                            if par == 1 and (qc * 4 + g) >= 5 and ((qc - 1) * 4 + 3) in pv_ev:
                                pass
                            if par == 1 and qc > 0 and g == 1:
                                nct.wait_ge(k.ve.h, pv_ev[(qc - 1) * 4 + 3])
                            for kt4 in range(4):
                                kt = g * 4 + kt4
                                ins = nct.matmul(
                                    ps[:, base + kt4 * 512:base + (kt4 + 1) * 512],
                                    kth[:, kt * 128:(kt + 1) * 128],
                                    qth[:, qc * 512:(qc + 1) * 512],
                                    start=True, stop=True)
                            t_mm = k.te.inc(ins)
                            ncs.wait_ge(k.te.h, t_mm)
                            t_exp_last = k.se.inc(ncs.activation(
                                pts_flat[:, g * 2048:(g + 1) * 2048],
                                ps[:, base:base + 2048], AF.Exp, bias=f32(-SHIFT)))
                            exp_done[gi] = t_exp_last
                        nct.wait_ge(k.se.h, t_exp_last)
                        for qt in range(4):
                            gq = qc * 4 + qt
                            po = ps[:, 2048 + (gq % 2) * 512:2048 + (gq % 2) * 512 + 132]
                            if gq >= 2:
                                nct.wait_ge(k.ve.h, pv_ev[gq - 2])
                            if gq == 0:
                                ncv.wait_ge(k.sA.h, oh_store)
                            for kt in range(16):
                                ins = nct.matmul(po[:, 0:129],
                                                 pts[:, kt, qt * 128:(qt + 1) * 128],
                                                 vp[:, kt, 0:129],
                                                 start=(kt == 0), stop=(kt == 15))
                            t_pv = k.te.inc(ins)
                            ncv.wait_ge(k.te.h, t_pv)
                            rc = recb[:, gq % 2:gq % 2 + 1]
                            k.ve.inc(ncv.reciprocal(rc, po[:, 128:129]))
                            ncv.wait_ge(k.ve.h, k.ve.n)
                            pv_ev[gq] = k.ve.inc(ncv.tensor_scalar(
                                oh[:, gq, :], po[:, 0:128], rc, None, ALU.mult))
                    sync.wait_ge(k.ve.h, pv_ev[15])
                    oh_store = dA(
                        attn[:, h * 128:(h + 1) * 128].rearrange("(j p) e -> p j e", p=128),
                        oh[:, :, :])
                t_attn = k.sA.n

                nc.gpsimd.wait_ge(k.sA.h, t_attn)
                for ds in range(0, 16, 4):
                    dW(wouta_sb[:, ds:ds + 4, :],
                       wouta_bf[ds * 128:(ds + 4) * 128, :].rearrange(
                           "(q p) e -> p q e", p=128))
                t_wo = k.sW.n
                for ds in range(16):
                    nc.scalar.wait_ge(k.sA.h, t_attn)
                    dB(aoT[:, ds, :], attn[:, ds * 128:(ds + 1) * 128],
                       transpose=True)
                t_aoT = k.sB.n
                nct.wait_ge(k.sW.h, t_wo)
                nct.wait_ge(k.sB.h, t_aoT)
                nct.wait_ge(k.ve.h, k.ve.n)
                ev_t = {}
                state = {}
                for tt in range(16):
                    par = tt % 2
                    pY = ps[:, par * 2048:par * 2048 + 2048]
                    if tt >= 2:
                        nct.wait_ge(k.ve.h, ev_t[tt - 2])
                    for ds in range(16):
                        for c in range(4):
                            nct.matmul(pY[:, c * 512:(c + 1) * 512],
                                       aoT[:, ds, tt * 128:(tt + 1) * 128],
                                       wouta_sb[:, ds, c * 512:(c + 1) * 512],
                                       start=(ds == 0), stop=False)
                    for c in range(4):
                        ins = nct.matmul(pY[:, c * 512:(c + 1) * 512],
                                         ones_bf[0:1, :],
                                         brow_bf[0:1, c * 512:(c + 1) * 512],
                                         start=False, stop=True)
                    t_mm = k.te.inc(ins)
                    ncv.wait_ge(k.te.h, t_mm)
                    ev_t[tt] = residual_evac_tile(pY, xsrc, xdst, tok0 + tt * 128,
                                                  par, None, state, do_ln=True)
                return k.sA.n

            def ffn(xsrc, xdst, xn_ready, emit_ln=False):
                chunks = []
                t0 = 0
                while t0 < NTOK:
                    cl = min(CH, NTOK - t0)
                    chunks.append((t0, cl))
                    t0 += cl
                t_gate_prev = 0
                dn_ev_last = 0
                ln_pending = None
                for ci, (c0, cl) in enumerate(chunks):
                    ntt = cl // 128
                    nc.scalar.wait_ge(k.te.h, t_gate_prev)
                    t_xnc = load_xnT(xnT_c, c0, cl, xn_ready)
                    if ln_pending is not None:
                        ln_range(xdst, ln_pending[0], ln_pending[1],
                                 x_ready=ln_pending[2])
                        ln_pending = None
                    ve_after_ln = k.ve.n
                    nct.wait_ge(k.sB.h, t_xnc)
                    if ci > 0:
                        nct.wait_ge(k.ve.h, dn_ev_last)
                    ght = {}
                    for ht in range(64):
                        par = ht % 2
                        if ht >= 2:
                            nc.gpsimd.wait_ge(k.te.h, ght[ht - 2][0])
                        else:
                            nc.gpsimd.wait_ge(k.te.h, t_gate_prev)
                        t_w = dW(wgu[par], wguT[ht].rearrange("g ds p e -> p g ds e"))
                        pg = ps[:, par * 1024:par * 1024 + cl]
                        pu = ps[:, 2048 + par * 1024:2048 + par * 1024 + cl]
                        nct.wait_ge(k.sW.h, t_w)
                        if ht >= 2:
                            nct.wait_ge(k.ve.h, ght[ht - 2][1])
                        nsplit = [(0, 512), (512, cl)] if cl > 512 else [(0, cl)]
                        for ds in range(16):
                            for (a, b) in nsplit:
                                ins = nct.matmul(pg[:, a:b],
                                                 wgu[par][:, 0, ds, :],
                                                 xnT_c[:, ds, a:b],
                                                 start=(ds == 0),
                                                 stop=(ds == 15 and b == cl))
                        t_g = k.te.inc(ins)
                        for ds in range(16):
                            for (a, b) in nsplit:
                                ins = nct.matmul(pu[:, a:b],
                                                 wgu[par][:, 1, ds, :],
                                                 xnT_c[:, ds, a:b],
                                                 start=(ds == 0),
                                                 stop=(ds == 15 and b == cl))
                        t_u = k.te.inc(ins)
                        ncs.wait_ge(k.te.h, t_g)
                        if ht >= 1:
                            ncs.wait_ge(k.ve.h, ght[ht - 1][1])
                        t_si = k.se.inc(ncs.activation(silu_sb[:, 0:cl], pg, AF.Silu))
                        ncv.wait_ge(k.se.h, t_si)
                        ncv.wait_ge(k.te.h, t_u)
                        t_gt = k.ve.inc(ncv.tensor_tensor(gTf[:, ht, 0:cl],
                                                          silu_sb[:, 0:cl], pu,
                                                          ALU.mult))
                        ght[ht] = (t_u, t_gt)
                    t_gate_prev = ght[63][0]
                    nct.wait_ge(k.ve.h, ght[63][1])
                    nct.wait_ge(k.ve.h, ght[62][1])
                    dn_mm = {}
                    dn_ev = {}
                    slice_store = {}
                    slice_ev = {}
                    for qd in range(4):
                        for hsg in range(16):
                            par = hsg % 2
                            gidx = qd * 16 + hsg
                            if gidx >= 2:
                                nc.gpsimd.wait_ge(k.te.h, dn_mm[gidx - 2])
                            else:
                                nc.gpsimd.wait_ge(k.te.h, ght[63][0])
                            t_wd = dW(wdt[par],
                                      wdT[hsg * 512:(hsg + 1) * 512,
                                          qd * 512:(qd + 1) * 512].rearrange(
                                              "(j p) e -> p j e", p=128))
                            nct.wait_ge(k.sW.h, t_wd)
                            if hsg == 0 and qd > 0:
                                nct.wait_ge(k.ve.h, dn_ev[(qd - 1, ntt - 1)])
                            for j in range(4):
                                hs = hsg * 4 + j
                                for tt in range(ntt):
                                    ins = nct.matmul(ps[:, tt * 512:(tt + 1) * 512],
                                                     gTf[:, hs, tt * 128:(tt + 1) * 128],
                                                     wdt[par][:, j, :],
                                                     start=(hs == 0), stop=(hs == 63))
                            dn_mm[gidx] = k.te.inc(ins)
                        t_q = dn_mm[qd * 16 + 15]
                        if qd == 0:
                            nc.scalar.wait_ge(k.ve.h, ve_after_ln)
                        for tt in range(ntt):
                            r0 = c0 + tt * 128
                            sl = (tt % 2, qd % 2)
                            xt = xf32[:, (tt % 2) * D + (qd % 2) * 512:
                                      (tt % 2) * D + (qd % 2) * 512 + 512]
                            if sl in slice_ev:
                                nc.scalar.wait_ge(k.ve.h, slice_ev[sl])
                            t_x = dB(xt, xsrc[r0:r0 + 128, qd * 512:(qd + 1) * 512])
                            ncv.wait_ge(k.sB.h, t_x)
                            ncv.wait_ge(k.te.h, t_q)
                            if (qd % 2,) in slice_store:
                                ncv.wait_ge(k.sA.h, slice_store[(qd % 2,)])
                            ev = evf32[:, (qd % 2) * 512:(qd % 2) * 512 + 512]
                            t_ev = k.ve.inc(ncv.tensor_tensor(
                                ev, ps[:, tt * 512:(tt + 1) * 512], xt, ALU.add))
                            dn_ev[(qd, tt)] = t_ev
                            slice_ev[sl] = t_ev
                            sync.wait_ge(k.ve.h, t_ev)
                            dA(xdst[r0:r0 + 128, qd * 512:(qd + 1) * 512], ev)
                            slice_store[(qd % 2,)] = k.sA.n
                    dn_ev_last = dn_ev[(3, ntt - 1)]
                    if emit_ln:
                        ln_pending = (c0, ntt, k.sA.n)
                if ln_pending is not None:
                    ln_range(xdst, ln_pending[0], ln_pending[1],
                             x_ready=ln_pending[2])
                return k.sA.n

            def attn_b(item, xsrc, xdst, xn_ready, stop=None):
                tok0 = item * LI
                t_xnT = load_xnT(xnT, tok0, LI, xn_ready)
                nct.wait_ge(k.sB.h, t_xnT)

                mm_done = {}
                ev_done = {}
                for ft in range(64):
                    par = ft % 2
                    wt = (wt0, wt1)[par]
                    if ft >= 2:
                        nc.gpsimd.wait_ge(k.te.h, mm_done[ft - 2])
                    t_w = dW(wt.rearrange("p (ds e) -> p ds e", e=128),
                             wqkvuT[ft].rearrange("ds p e -> p ds e"))
                    nct.wait_ge(k.sW.h, t_w)
                    for qc in range(4):
                        gi = ft * 4 + qc
                        pb = ps[:, (gi % 2) * 512:(gi % 2) * 512 + 512]
                        if gi >= 2:
                            nct.wait_ge(k.se.h, ev_done[gi - 2])
                        for ds in range(16):
                            ins = nct.matmul(pb, wt[:, ds * 128:(ds + 1) * 128],
                                             xnT[:, ds, qc * 512:(qc + 1) * 512],
                                             start=(ds == 0), stop=(ds == 15))
                        t_mm = k.te.inc(ins)
                        if qc == 3:
                            mm_done[ft] = t_mm
                        ncs.wait_ge(k.te.h, t_mm)
                        func = AF.Silu if ft < 32 else AF.Copy
                        ev_done[gi] = k.se.inc(ncs.activation(
                            evst[:, qc * 512:(qc + 1) * 512], pb, func))
                    sync.wait_ge(k.se.h, ev_done[ft * 4 + 3])
                    t_row = dA(qkvuTb[ft * 128:(ft + 1) * 128, :], evst[:, :])
                    ncs.wait_ge(k.sA.h, t_row)
                t_qkvuT = k.sA.n
                if stop == "b2a":
                    barrier()
                    return k.sA.n

                tst_store = {}
                for j, (src_r, dst) in enumerate(((D, kUb), (2 * D, vUb))):
                    for ts in range(16):
                        idx = j * 16 + ts
                        st = (tst0, tst1)[idx % 2]
                        if idx >= 2:
                            nc.scalar.wait_ge(k.sA.h, tst_store[idx - 2])
                        nc.scalar.wait_ge(k.sA.h, t_qkvuT)
                        dB(st[:, :], qkvuTb[src_r:src_r + D,
                                            ts * 128:(ts + 1) * 128], transpose=True)
                        sync.wait_ge(k.sB.h, k.sB.n)
                        dA(dst[ts * 128:(ts + 1) * 128, :], st[:, :])
                        tst_store[idx] = k.sA.n
                t_kvU = k.sA.n
                if stop == "b2a5":
                    barrier()
                    return k.sA.n

                psO = ps[:, 0:256]
                psQK = (ps[:, 512:768], ps[:, 768:1024])
                psRC = ps[:, 1024:1040]
                psKV = ps[:, 1536:1664]
                for h in range(SCAN_NH):
                    sync.wait_ge(k.te.h, k.te.n)
                    sync.wait_ge(k.ve.h, k.ve.n)
                    sync.wait_ge(k.sA.h, t_kvU)
                    dA(qtb[:, :], qkvuTb[h * 128:(h + 1) * 128, :])
                    dA(ktb[:, :], qkvuTb[D + h * 128:D + (h + 1) * 128, :])
                    dA(utb[:, :], qkvuTb[3 * D + h * 128:3 * D + (h + 1) * 128, :])
                    dA(ks3[:, :, :], kUb[:, h * 128:(h + 1) * 128].rearrange(
                        "(j p) e -> p j e", p=128))
                    dA(vs3[:, :, :], vUb[:, h * 128:(h + 1) * 128].rearrange(
                        "(j p) e -> p j e", p=128))
                    t_ld = k.sA.n
                    scol = slope_sb[:, h:h + 1]
                    ncv.wait_ge(k.te.h, k.te.n)
                    ncv.wait_ge(k.ve.h, k.ve.n)
                    t1 = k.ve.inc(ncv.tensor_scalar(dec_f[:, :], iota_sb[:, :],
                                                    scol, None, ALU.mult))
                    ncs.wait_ge(k.ve.h, t1)
                    ncs.wait_ge(k.se.h, k.se.n)
                    tq = k.se.inc(ncs.activation(qdec[:, :], dec_f[:, :], AF.Exp,
                                                 scale=f32(-1.0)))
                    for st in range(2):
                        ncv.wait_ge(k.se.h, k.se.n)
                        t2 = k.ve.inc(ncv.tensor_scalar(dec_f[:, :],
                                                        idxt_sb[:, st, :],
                                                        scol, None, ALU.mult))
                        ncs.wait_ge(k.ve.h, t2)
                        t3 = k.se.inc(ncs.activation(dtmp[:, :], dec_f[:, :],
                                                     AF.Exp, scale=f32(-1.0)))
                        ncv.wait_ge(k.se.h, t3)
                        k.ve.inc(ncv.tensor_tensor(ddm[:, st, :], dtmp[:, :],
                                                   maskt_sb[:, st, :], ALU.mult))
                    ncv.wait_ge(k.se.h, k.se.n)
                    t4 = k.ve.inc(ncv.tensor_scalar(dec_f[:, 0:2], iotc_sb[:, :],
                                                    256.0, scol,
                                                    ALU.subtract, ALU.mult))
                    ncs.wait_ge(k.ve.h, t4)
                    t5 = k.se.inc(ncs.activation(kdec[:, :], dec_f[:, 0:2], AF.Exp))
                    t5b = k.se.inc(ncs.activation(bdcol[:, :], scol, AF.Exp,
                                                  scale=f32(-256.0)))
                    ncv.wait_ge(k.se.h, t5b)
                    k.ve.inc(ncv.memset(kv_sb[:, :], 0.0))
                    t_kvbf = k.ve.inc(ncv.memset(kv_bf[:, :], 0.0))
                    nct.wait_ge(k.sA.h, t_ld)
                    if h == 0:
                        nct.wait_ge(k.se.h, k.se.n)
                    t_a = k.ve.n
                    t_sq = k.ve.n
                    t_gt = k.ve.n
                    t_cp_prev = k.se.n
                    for blk in range(SCAN_NB):
                        m0 = blk * BLK
                        nct.wait_ge(k.ve.h, t_a)
                        for st in range(2):
                            t_qk = k.te.inc(nct.matmul(
                                psQK[st], ktb[:, m0 + st * 128:m0 + st * 128 + 128],
                                qtb[:, m0:m0 + BLK], start=True, stop=True))
                        ncv.wait_ge(k.te.h, t_qk)
                        tqs = k.ve.inc(ncv.tensor_tensor(qsc[:, :],
                                                         qtb[:, m0:m0 + BLK],
                                                         qdec[:, :], ALU.mult))
                        for st in range(2):
                            t_a = k.ve.inc(ncv.tensor_tensor(asb[:, st, :],
                                                             psQK[st],
                                                             ddm[:, st, :],
                                                             ALU.mult))
                        for st in range(2):
                            t_kt = k.ve.inc(ncv.tensor_scalar(
                                ksc[:, st, :], ks3[:, 2 * blk + st, :],
                                kdec[:, st:st + 1], None, ALU.mult))
                        nct.wait_ge(k.ve.h, tqs)
                        nct.wait_ge(k.ve.h, t_kvbf)
                        nct.wait_ge(k.se.h, t_cp_prev)
                        nct.matmul(psO, kv_bf[:, :], qsc[:, :], start=True, stop=False)
                        nct.wait_ge(k.ve.h, t_a)
                        for st in range(2):
                            ins = nct.matmul(psO, vs3[:, 2 * blk + st, :],
                                             asb[:, st, :],
                                             start=False, stop=(st == 1))
                        t_o = k.te.inc(ins)
                        ncs.wait_ge(k.te.h, t_o)
                        ncs.wait_ge(k.ve.h, t_gt)
                        t_cp = k.se.inc(ncs.activation(o_sb[:, :], psO, AF.Copy))
                        ncv.wait_ge(k.se.h, t_cp)
                        t_gt = k.ve.inc(ncv.tensor_tensor(gTb[:, h, m0:m0 + BLK],
                                                          o_sb[:, :],
                                                          utb[:, m0:m0 + BLK],
                                                          ALU.mult))
                        t_sq = k.ve.inc(ncv.tensor_tensor(sq_bf[:, :], o_sb[:, :],
                                                          o_sb[:, :], ALU.mult))
                        nct.wait_ge(k.ve.h, t_sq)
                        for st in range(2):
                            c = 2 * blk + st
                            nct.matmul(psRC[:, c:c + 1],
                                       sq_bf[:, st * 128:st * 128 + 128],
                                       ones_col[:, 0:1],
                                       start=(h == 0 and blk == 0 and st == 0),
                                       stop=(h == SCAN_NH - 1 and blk == SCAN_NB - 1 and st == 1))
                        nct.wait_ge(k.ve.h, t_kt)
                        for st in range(2):
                            ins = nct.matmul(psKV, ksc[:, st, :],
                                             vs3[:, 2 * blk + st, :],
                                             start=(st == 0), stop=(st == 1))
                        t_kv = k.te.inc(ins)
                        ncv.wait_ge(k.te.h, t_kv)
                        k.ve.inc(ncv.scalar_tensor_tensor(kv_sb[:, :], kv_sb[:, :],
                                                          bdcol[:, 0:1], psKV,
                                                          ALU.mult, ALU.add))
                        ncv.wait_ge(k.ve.h, k.ve.n)
                        t_kvbf = k.ve.inc(ncv.tensor_copy(kv_bf[:, :], kv_sb[:, :]))
                        t_cp_prev = t_cp
                t_rc_done = k.te.n
                if stop == "scan":
                    barrier()
                    return k.sA.n

                ncs.wait_ge(k.te.h, t_rc_done)
                ncs.wait_ge(k.se.h, k.se.n)
                t_l = k.se.inc(ncs.activation(lnc[:, :], psRC, AF.Ln,
                                              bias=f32(EPS_RMS), scale=f32(1.0 / D)))
                ncs.wait_ge(k.se.h, t_l)
                t_rsc = k.se.inc(ncs.activation(rstdc[:, :], lnc[:, :], AF.Exp,
                                                scale=f32(-0.5)))

                nc.scalar.wait_ge(k.ve.h, k.ve.n)
                for hs in range(16):
                    dB(woutbT[:, hs, :], woutb_bf[:, hs * 128:(hs + 1) * 128],
                       transpose=True)
                t_wob = k.sB.n
                nct.wait_ge(k.sB.h, t_wob)
                nct.wait_ge(k.ve.h, k.ve.n)
                nct.wait_ge(k.se.h, t_rsc)
                ev_t = {}
                state = {}
                for tt in range(16):
                    par = tt % 2
                    pY = ps[:, par * 2048:par * 2048 + 2048]
                    if tt >= 2:
                        nct.wait_ge(k.ve.h, ev_t[tt - 2])
                    for hs in range(16):
                        for c in range(4):
                            ins = nct.matmul(pY[:, c * 512:(c + 1) * 512],
                                             gTb[:, hs, tt * 128:(tt + 1) * 128],
                                             woutbT[:, hs, c * 512:(c + 1) * 512],
                                             start=(hs == 0), stop=(hs == 15))
                    t_mm = k.te.inc(ins)
                    ncv.wait_ge(k.te.h, t_mm)
                    ncv.wait_ge(k.se.h, t_rsc)
                    ev_t[tt] = residual_evac_tile(pY, xsrc, xdst, tok0 + tt * 128,
                                                  par, rstdc[:, tt:tt + 1],
                                                  state, do_ln=True)
                return k.sA.n

            def stop(name):
                return stop_after == name

            while True:
                if stop("prepass"):
                    break
                t_ln0 = ln_range(x_in, 0, NTI)
                barrier()
                if stop("ln0"):
                    break
                attn_a(0, x_in, xs1, t_ln0)
                barrier()
                if stop("attn_a0"):
                    break
                t_ln1 = ln_range(x_in, LI, NTI)
                barrier()
                attn_a(1, x_in, xs1, t_ln1)
                barrier()
                if stop("attn_a1"):
                    break
                ffn(xs1, xs2, k.sA.n, emit_ln=True)
                t_ln2 = k.sA.n
                barrier()
                if stop("ffn1"):
                    break
                if stop("ln2"):
                    break
                if stop_after in ("b2a", "b2a5", "scan"):
                    attn_b(0, xs2, xs3, t_ln2, stop=stop_after)
                    break
                attn_b(0, xs2, xs3, t_ln2)
                barrier()
                if stop("attn_b0"):
                    break
                attn_b(1, xs2, xs3, t_ln2)
                barrier()
                if stop("attn_b1"):
                    break
                ffn(xs3, y, k.sA.n)
                barrier()
                break

    ctx.close()
    return nc


def kernel(L0_attn_a_x, L0_attn_a_w_qkv, L0_attn_a_b_qkv, L0_attn_a_w_out,
           L0_attn_a_b_out, L0_attn_b_slope_rate, L0_attn_b_W_qkvu,
           L0_attn_b_W_out, L0_ffn_w_gate, L0_ffn_w_up, L0_ffn_w_down):
    x = np.ascontiguousarray(np.asarray(L0_attn_a_x, dtype=np.float32))
    wqkv = np.ascontiguousarray(np.asarray(L0_attn_a_w_qkv, dtype=np.float32))
    bqkv = np.ascontiguousarray(np.asarray(L0_attn_a_b_qkv, dtype=np.float32))
    wouta_ = np.ascontiguousarray(np.asarray(L0_attn_a_w_out, dtype=np.float32))
    bouta = np.ascontiguousarray(np.asarray(L0_attn_a_b_out, dtype=np.float32))
    slope_ = np.ascontiguousarray(np.asarray(L0_attn_b_slope_rate, dtype=np.float32))
    wqkvu_ = np.ascontiguousarray(np.asarray(L0_attn_b_W_qkvu, dtype=np.float32))
    woutb_ = np.ascontiguousarray(np.asarray(L0_attn_b_W_out, dtype=np.float32))
    wgate = np.ascontiguousarray(np.asarray(L0_ffn_w_gate, dtype=np.float32))
    wup = np.ascontiguousarray(np.asarray(L0_ffn_w_up, dtype=np.float32))
    wdown = np.ascontiguousarray(np.asarray(L0_ffn_w_down, dtype=np.float32))

    bqkv_cols = np.ascontiguousarray(bqkv.reshape(48, 128).T)
    bout_row = np.ascontiguousarray(bouta.reshape(1, D))
    slope_b = np.ascontiguousarray(np.broadcast_to(slope_.reshape(1, NH), (128, NH)))
    ar = np.arange(1, BLK + 1, dtype=np.float32)
    iota_row = np.ascontiguousarray(np.broadcast_to(ar[None, :], (128, BLK)))
    iota_col2 = np.ascontiguousarray(ar.reshape(2, 128).T)
    sm = ar[None, :] - ar[:, None]
    idxt = np.ascontiguousarray(np.maximum(sm, 0.0).astype(np.float32))
    maskt = np.ascontiguousarray((sm >= 0).astype(np.float32))

    nc = build()
    in_maps = []
    for c in range(8):
        xi = np.ascontiguousarray(x[2 * c:2 * c + 2].reshape(NTOK, D))
        in_maps.append(dict(
            x=xi, wqkv=wqkv, bqkv_cols=bqkv_cols,
            bout_row=bout_row, wouta=wouta_, wqkvu=wqkvu_, woutb=woutb_,
            slope=slope_b, wg=wgate, wu=wup, wd=wdown, iota_row=iota_row,
            iota_col2=iota_col2, idxt=idxt, maskt=maskt))
    res = run_bass_kernel_spmd(nc, in_maps, core_ids=list(range(8)))
    out = np.empty((16, LI, D), dtype=np.float32)
    for c in range(8):
        out[2 * c:2 * c + 2] = res.results[c]["y"].reshape(2, LI, D)
    return out


if __name__ == "__main__":
    import reference as R
    d = R.setup_inputs(0)
    inputs = {kk: np.asarray(v) for kk, v in d.items()}
    out = kernel(**inputs)
    ref = np.asarray(R.reference(**d))
    num = float(((out - ref) ** 2).mean())
    den = float((ref ** 2).mean())
    print("resvar:", num / den)


# version 11
# speedup vs baseline: 1.1116x; 1.1116x over previous
import collections
import contextlib

import numpy as np
import concourse.bass as bass
import concourse.mybir as mybir
from concourse.bass_utils import run_bass_kernel_spmd

F32 = mybir.dt.float32
BF16 = mybir.dt.bfloat16
AF = mybir.ActivationFunctionType
ALU = mybir.AluOpType

D = 2048
LI = 2048
ITEMS = 2
NTOK = LI * ITEMS
NTI = LI // 128
NTT = NTOK // 128
HF = 8192
NH = 16
HD = 128
BLK = 256
NB = LI // BLK
CH = 768
SHIFT = 12.0
QK_SCALE = float(1.0 / np.sqrt(np.float32(128.0)))
EPS_LN = 1e-5
EPS_RMS = 1e-6
f32 = np.float32
SCAN_NH = NH
SCAN_NB = NB


class Sem:
    def __init__(self, ctx, nc, name):
        self.h = ctx.enter_context(nc.semaphore(name))
        self.n = 0

    def inc(self, instr, k=1):
        instr.then_inc(self.h, k)
        self.n += k
        return self.n


def build(debug_names=(), stop_after=None):
    nc = bass.Bass("TRN2", target_bir_lowering=False, debug=False)
    ctx = contextlib.ExitStack()

    def dram(name, shape, dtype, kind):
        if kind == "Internal" and name in debug_names:
            kind = "ExternalOutput"
        return nc.dram_tensor(name, shape, dtype, kind=kind).ap()

    x_in = dram("x", [NTOK, D], F32, "ExternalInput")
    wqkv = dram("wqkv", [D, 3 * D], F32, "ExternalInput")
    bqkv_cols = dram("bqkv_cols", [128, 48], F32, "ExternalInput")
    bout_row = dram("bout_row", [1, D], F32, "ExternalInput")
    wouta = dram("wouta", [D, D], F32, "ExternalInput")
    wqkvu = dram("wqkvu", [HF, D], F32, "ExternalInput")
    woutb = dram("woutb", [D, D], F32, "ExternalInput")
    slope = dram("slope", [128, NH], F32, "ExternalInput")
    wg = dram("wg", [HF, D], F32, "ExternalInput")
    wu = dram("wu", [HF, D], F32, "ExternalInput")
    wd = dram("wd", [D, HF], F32, "ExternalInput")
    iota_row = dram("iota_row", [128, BLK], F32, "ExternalInput")
    iota_col2 = dram("iota_col2", [128, 2], F32, "ExternalInput")
    idxt = dram("idxt", [BLK, BLK], F32, "ExternalInput")
    maskt = dram("maskt", [BLK, BLK], F32, "ExternalInput")
    y = dram("y", [NTOK, D], F32, "ExternalOutput")

    xs1 = dram("xs1", [NTOK, D], F32, "Internal")
    xs2 = dram("xs2", [NTOK, D], F32, "Internal")
    xs3 = dram("xs3", [NTOK, D], F32, "Internal")
    xnscr = dram("xnscr", [NTOK, D], BF16, "Internal")
    wqkv_bf = dram("wqkv_bf", [D, 3 * D], BF16, "Internal")
    wouta_bf = dram("wouta_bf", [D, D], BF16, "Internal")
    woutb_bf = dram("woutb_bf", [D, D], BF16, "Internal")
    wqkvu_bf = dram("wqkvu_bf", [HF, D], BF16, "Internal")
    wg_bf = dram("wg_bf", [HF, D], BF16, "Internal")
    wu_bf = dram("wu_bf", [HF, D], BF16, "Internal")
    wd_bf = dram("wd_bf", [D, HF], BF16, "Internal")
    wqkvuT = dram("wqkvuT", [64, 16, 128, 128], BF16, "Internal")
    wguT = dram("wguT", [64, 2, 16, 128, 128], BF16, "Internal")
    wdT = dram("wdT", [HF, D], BF16, "Internal")
    qkvT = dram("qkvT", [2, 3 * D, LI], BF16, "Internal")
    vU = dram("vU", [2, LI, D], BF16, "Internal")
    attn = dram("attn", [2, LI, D], BF16, "Internal")
    qkvuTb = dram("qkvuTb", [4 * D, LI], BF16, "Internal")
    kUb = dram("kUb", [LI, D], BF16, "Internal")
    vUb = dram("vUb", [LI, D], BF16, "Internal")

    BIGC = 65536
    big = nc.alloc_sbuf_tensor("big", [128, BIGC], BF16)
    SMC = 14464
    small = nc.alloc_sbuf_tensor("small", [128, SMC], BF16)
    xf32 = nc.alloc_sbuf_tensor("xf32", [128, 2 * D], F32)
    evf32 = nc.alloc_sbuf_tensor("evf32", [128, D], F32)
    xnorm = nc.alloc_sbuf_tensor("xnorm", [128, D], BF16)
    bn24 = nc.alloc_sbuf_tensor("bn24", [128, 24], F32)
    mvb = nc.alloc_sbuf_tensor("mvb", [128, 4], F32)
    lnvb = nc.alloc_sbuf_tensor("lnvb", [128, 2], F32)
    rstdb = nc.alloc_sbuf_tensor("rstdb", [128, 2], F32)
    bcols = nc.alloc_sbuf_tensor("bcols", [128, 48], F32)
    brow_bf = nc.alloc_sbuf_tensor("brow_bf", [1, D], BF16)
    ones_bf = nc.alloc_sbuf_tensor("ones_bf", [1, 128], BF16)
    ones_col = nc.alloc_sbuf_tensor("ones_col", [128, 1], BF16)
    slope_sb = nc.alloc_sbuf_tensor("slope_sb", [128, NH], F32)
    iota_sb = nc.alloc_sbuf_tensor("iota_sb", [128, BLK], F32)
    iotc_sb = nc.alloc_sbuf_tensor("iotc_sb", [128, 2], F32)
    idxt_sb = nc.alloc_sbuf_tensor("idxt_sb", [128, 2, BLK], BF16)
    maskt_sb = nc.alloc_sbuf_tensor("maskt_sb", [128, 2, BLK], BF16)
    rstdc = nc.alloc_sbuf_tensor("rstdc", [128, NTI], F32)
    lnc = nc.alloc_sbuf_tensor("lnc", [128, NTI], F32)
    kv_sb = nc.alloc_sbuf_tensor("kv_sb", [128, 128], F32)
    kv_bf = nc.alloc_sbuf_tensor("kv_bf", [128, 128], BF16)
    sq_bf = nc.alloc_sbuf_tensor("sq_bf", [128, BLK], BF16)
    o_sb = nc.alloc_sbuf_tensor("o_sb", [128, BLK], BF16)
    dec_f = nc.alloc_sbuf_tensor("dec_f", [128, BLK], F32)
    qdec = nc.alloc_sbuf_tensor("qdec", [128, BLK], BF16)
    ddm = nc.alloc_sbuf_tensor("ddm", [128, 2, BLK], BF16)
    dtmp = nc.alloc_sbuf_tensor("dtmp", [128, BLK], BF16)
    kdec = nc.alloc_sbuf_tensor("kdec", [128, 2], F32)
    bdcol = nc.alloc_sbuf_tensor("bdcol", [128, 1], F32)
    ksc = nc.alloc_sbuf_tensor("ksc", [128, 2, 128], BF16)
    qsc = nc.alloc_sbuf_tensor("qsc", [128, BLK], BF16)
    asb = nc.alloc_sbuf_tensor("asb", [128, 2, BLK], BF16)
    recb = nc.alloc_sbuf_tensor("recb", [128, 2], F32)
    tstA = nc.alloc_sbuf_tensor("tstA", [128, 2048], BF16)
    tstB = nc.alloc_sbuf_tensor("tstB", [128, 2048], BF16)

    ps = ctx.enter_context(nc.psum_tensor("ps", [128, 4096], F32))

    k = K = type("KK", (), {})()
    k.sW = Sem(ctx, nc, "sW")
    k.sA = Sem(ctx, nc, "sA")
    k.sB = Sem(ctx, nc, "sB")
    k.te = Sem(ctx, nc, "te")
    k.ve = Sem(ctx, nc, "ve")
    k.se = Sem(ctx, nc, "se")

    def dW(out, in_, **kw):
        return k.sW.inc(nc.gpsimd.dma_start(out=out, in_=in_, **kw), 16)

    def dA(out, in_, **kw):
        return k.sA.inc(nc.sync.dma_start(out=out, in_=in_, **kw), 16)

    def dB(out, in_, **kw):
        return k.sB.inc(nc.scalar.dma_start(out=out, in_=in_, **kw), 16)

    nct = nc.tensor
    ncv = nc.vector
    ncs = nc.scalar
    bgc = collections.deque()

    def pump(n=1):
        for _ in range(n):
            if not bgc:
                return
            bgc.popleft()()

    def barrier():
        for eng in (nct, ncv, ncs, nc.gpsimd, nc.sync):
            for s in (k.sW, k.sA, k.sB, k.te, k.ve, k.se):
                eng.wait_ge(s.h, s.n)

    def soft():
        nct.wait_ge(k.ve.h, k.ve.n)
        nct.wait_ge(k.se.h, k.se.n)
        for eng in (nc.sync, nc.scalar, nc.gpsimd):
            eng.wait_ge(k.te.h, k.te.n)
            eng.wait_ge(k.ve.h, k.ve.n)
            eng.wait_ge(k.se.h, k.se.n)
        ncv.wait_ge(k.sA.h, k.sA.n)
        ncv.wait_ge(k.sB.h, k.sB.n)

    xnT = big[:, 0:32768].rearrange("p (ds t) -> p ds t", t=2048)
    xnT1 = big[:, 32768:65536].rearrange("p (ds t) -> p ds t", t=2048)
    pts = big[:, 32768:40960].rearrange("p (kt q) -> p kt q", q=512)
    stage0 = big[:, 40960:49152]
    stage1 = big[:, 49152:57344]
    wouta_sb = big[:, 0:32768].rearrange("p (ds e) -> p ds e", e=2048)
    aoT = big[:, 32768:65536].rearrange("p (ds t) -> p ds t", t=2048)
    gTf = big[:, 0:49152].rearrange("p (h t) -> p h t", t=CH)
    xnT_c = big[:, 49152:61440].rearrange("p (ds t) -> p ds t", t=CH)
    gTb = big[:, 0:32768].rearrange("p (h t) -> p h t", t=2048)
    woutbT = big[:, 32768:65536].rearrange("p (h e) -> p h e", e=2048)

    wt0 = small[:, 0:2048]
    wt1 = small[:, 2048:4096]
    evst = small[:, 4096:6144]
    evst1 = small[:, 6144:8192]
    tst0 = tstA[:, :]
    tst1 = tstB[:, :]
    kth2 = [small[:, 0:2048], small[:, 4096:6144]]
    qth2 = [small[:, 2048:4096], small[:, 6144:8192]]
    vp2 = [small[:, 8192:10304].rearrange("p (kt e) -> p kt e", e=132),
           small[:, 10304:12416].rearrange("p (kt e) -> p kt e", e=132)]
    oh = small[:, 12416:14464].rearrange("p (j e) -> p j e", e=128)
    wgu = [small[:, 0:4096].rearrange("p (g ds e) -> p g ds e", g=2, e=128),
           small[:, 4096:8192].rearrange("p (g ds e) -> p g ds e", g=2, e=128)]
    silu_sb = small[:, 8192:9088]
    wdt = [small[:, 9216:11264].rearrange("p (j e) -> p j e", e=512),
           small[:, 11264:13312].rearrange("p (j e) -> p j e", e=512)]
    qtb = small[:, 0:2048]
    ktb = small[:, 2048:4096]
    utb = small[:, 4096:6144]
    ks3 = small[:, 6144:8192].rearrange("p (j e) -> p j e", e=128)
    vs3 = small[:, 8192:10240].rearrange("p (j e) -> p j e", e=128)

    with nc.Block() as block:

        @block.sync
        def _(sync):
            dW(bcols[:, :], bqkv_cols[:, :])
            dW(brow_bf[0:1, :], bout_row[0:1, :])
            dW(slope_sb[:, :], slope[:, :])
            dW(iota_sb[:, :], iota_row[:, :])
            dW(iotc_sb[:, :], iota_col2[:, :])
            dW(idxt_sb[:, :, :], idxt.rearrange("(st p) m -> p st m", p=128))
            dW(maskt_sb[:, :, :], maskt.rearrange("(st p) m -> p st m", p=128))
            t_cst = k.sW.n
            ncv.wait_ge(k.sW.h, t_cst)
            ncv.memset(ones_bf[:, :], 1.0)
            ncv.memset(ones_col[:, :], 1.0)
            ncv.memset(ones_col[:, :], 1.0)

            def cast_w(dst, src, rows, blk_rows):
                for r0 in range(0, rows, blk_rows):
                    dW(dst[r0:r0 + blk_rows, :], src[r0:r0 + blk_rows, :])
                return k.sW.n

            t_wqkv_cast = cast_w(wqkv_bf, wqkv, D, 256)
            t_wouta_cast = cast_w(wouta_bf, wouta, D, 512)
            cast_t = {}

            def enq_cast(key, dst, src, rows, blk_rows):
                def mk(r0):
                    def go():
                        dW(dst[r0:r0 + blk_rows, :], src[r0:r0 + blk_rows, :])
                        cast_t[key] = k.sW.n
                    return go
                for r0 in range(0, rows, blk_rows):
                    bgc.append(mk(r0))

            enq_cast("wg", wg_bf, wg, HF, 512)
            enq_cast("wu", wu_bf, wu, HF, 512)
            enq_cast("wd", wd_bf, wd, D, 128)
            enq_cast("wqkvu", wqkvu_bf, wqkvu, HF, 512)
            enq_cast("woutb", woutb_bf, woutb, D, 512)

            store_t = {}
            pp_i = [0]

            def pp_step(extra, emit_tr, emit_st):
                i = pp_i[0]
                st = (tst0, tst1)[i % 2]
                if i >= 2:
                    nc.scalar.wait_ge(k.sA.h, store_t[i - 2])
                if extra is not None:
                    nc.scalar.wait_ge(extra[0], extra[1])
                emit_tr(st)
                sync.wait_ge(k.sB.h, k.sB.n)
                emit_st(st)
                store_t[i] = k.sA.n
                pp_i[0] = i + 1

            def prep_all():
                for key, dst_pair in (("wg", 0), ("wu", 1)):
                    for ds in range(16):
                        for q in range(4):
                            pp_step(
                                (k.sW.h, cast_t[key]),
                                lambda st, ds=ds, q=q: dB(
                                    st[:, :],
                                    (wg_bf if key == "wg" else wu_bf)[
                                        q * 2048:(q + 1) * 2048,
                                        ds * 128:(ds + 1) * 128], transpose=True),
                                lambda st, ds=ds, q=q, dp=dst_pair: dA(
                                    wguT[q * 16:(q + 1) * 16, dp, ds].rearrange(
                                        "ft p e -> p ft e"),
                                    st.rearrange("p (ft e) -> p ft e", e=128)))
                for hs in range(64):
                    pp_step(
                        (k.sW.h, cast_t["wd"]),
                        lambda st, hs=hs: dB(st[:, 0:2048],
                                             wd_bf[:, hs * 128:(hs + 1) * 128],
                                             transpose=True),
                        lambda st, hs=hs: dA(wdT[hs * 128:(hs + 1) * 128, :],
                                             st[:, 0:2048]))
                for ds in range(16):
                    for q in range(4):
                        pp_step(
                            (k.sW.h, cast_t["wqkvu"]),
                            lambda st, ds=ds, q=q: dB(
                                st[:, :], wqkvu_bf[q * 2048:(q + 1) * 2048,
                                                   ds * 128:(ds + 1) * 128],
                                transpose=True),
                            lambda st, ds=ds, q=q: dA(
                                wqkvuT[q * 16:(q + 1) * 16, ds].rearrange(
                                    "ft p e -> p ft e"),
                                st.rearrange("p (ft e) -> p ft e", e=128)))
                return k.sA.n

            def ln_core(src_sb, dst_r0, par, eps):
                xnt = xnorm[:, :]
                mv = mvb[:, par * 2:par * 2 + 2]
                lnv = lnvb[:, par:par + 1]
                rst = rstdb[:, par:par + 1]
                ncv.wait_ge(k.ve.h, k.ve.n)
                for i in range(4):
                    ins = ncv.bn_stats(bn24[:, 6 * i:6 * i + 6],
                                       src_sb[:, 512 * i:512 * (i + 1)])
                k.ve.inc(ins)
                ncv.wait_ge(k.ve.h, k.ve.n)
                t_mv = k.ve.inc(ncv.bn_aggr(mv, bn24[:, :]))
                ncs.wait_ge(k.ve.h, t_mv)
                ncs.wait_ge(k.se.h, k.se.n)
                t_ln = k.se.inc(ncs.activation(lnv, mv[:, 1:2], AF.Ln, bias=f32(eps)))
                ncs.wait_ge(k.se.h, t_ln)
                t_rs = k.se.inc(ncs.activation(rst, lnv, AF.Exp, scale=f32(-0.5)))
                ncv.wait_ge(k.se.h, t_rs)
                t_xn = k.ve.inc(ncv.tensor_scalar(xnt, src_sb, mv[:, 0:1], rst,
                                                  ALU.subtract, ALU.mult))
                sync.wait_ge(k.ve.h, t_xn)
                t_st = dA(xnscr[dst_r0:dst_r0 + 128, :], xnt)
                return t_st, t_xn

            xn_store = {0: 0}

            def ln_range(src, tok0, ntiles, eps=EPS_LN, x_ready=0):
                nc.scalar.wait_ge(k.sA.h, x_ready)
                prev = {}
                for tt in range(ntiles):
                    par = tt % 2
                    r0 = tok0 + tt * 128
                    xt = xf32[:, par * D:(par + 1) * D]
                    if tt >= 2:
                        nc.scalar.wait_ge(k.ve.h, prev[tt - 2][1])
                    ncv.wait_ge(k.sA.h, xn_store[0])
                    tld = dB(xt, src[r0:r0 + 128, :])
                    ncv.wait_ge(k.sB.h, tld)
                    prev[tt] = ln_core(xt, r0, par, eps)
                    xn_store[0] = prev[tt][0]
                return k.sA.n

            def load_xnT(dst3, tok0, ntok, xn_ready_sA):
                for ds in range(16):
                    nc.scalar.wait_ge(k.sA.h, xn_ready_sA)
                    dB(dst3[:, ds, 0:ntok],
                       xnscr[tok0:tok0 + ntok, ds * 128:(ds + 1) * 128],
                       transpose=True)
                return k.sB.n

            def residual_evac_tile(pY, xsrc, xdst, r0, par, rstd_col,
                                   state, do_ln):
                xt = xf32[:, par * D:(par + 1) * D]
                t_x = dB(xt, xsrc[r0:r0 + 128, :])
                ncv.wait_ge(k.sB.h, t_x)
                ncv.wait_ge(k.sA.h, state.get("ev_store", 0))
                ncv.wait_ge(k.ve.h, k.ve.n)
                if rstd_col is None:
                    t_ev = k.ve.inc(ncv.tensor_tensor(evf32[:, :], pY, xt, ALU.add))
                else:
                    t_ev = k.ve.inc(ncv.scalar_tensor_tensor(
                        evf32[:, :], pY, rstd_col, xt, ALU.mult, ALU.add))
                sync.wait_ge(k.ve.h, t_ev)
                dA(xdst[r0:r0 + 128, :], evf32[:, :])
                state["ev_store"] = k.sA.n
                if do_ln:
                    ncv.wait_ge(k.sA.h, xn_store[0])
                    t_st, _ = ln_core(evf32[:, :], r0, par, EPS_LN)
                    xn_store[0] = t_st
                return t_ev

            def attn_a_all(xsrc, xdst, xn_ready):
                t_x0 = load_xnT(xnT, 0, LI, xn_ready)
                t_xnT = load_xnT(xnT1, LI, LI, xn_ready)
                nct.wait_ge(k.sB.h, t_xnT)

                wt_dma = {}
                mm_done = {}
                ev_done = {}
                row_t = {}
                nc.gpsimd.wait_ge(k.sW.h, t_wqkv_cast)
                gi = 0
                for ft in range(48):
                    par = ft % 2
                    wt = (wt0, wt1)[par]
                    if ft >= 2:
                        nc.gpsimd.wait_ge(k.te.h, mm_done[ft - 2])
                    wt_dma[ft] = dW(
                        wt.rearrange("p (ds e) -> p ds e", e=128),
                        wqkv_bf[:, ft * 128:(ft + 1) * 128].rearrange(
                            "(ds p) e -> p ds e", p=128))
                    nct.wait_ge(k.sW.h, wt_dma[ft])
                    pump(2)
                    for it in range(2):
                        xn_it = (xnT, xnT1)[it]
                        ev_it = (evst, evst1)[it]
                        for qc in range(4):
                            pb = ps[:, (gi % 2) * 512:(gi % 2) * 512 + 512]
                            if gi >= 2:
                                nct.wait_ge(k.ve.h, ev_done[gi - 2])
                            for ds in range(16):
                                ins = nct.matmul(pb, wt[:, ds * 128:(ds + 1) * 128],
                                                 xn_it[:, ds, qc * 512:(qc + 1) * 512],
                                                 start=(ds == 0), stop=(ds == 15))
                            t_mm = k.te.inc(ins)
                            if it == 1 and qc == 3:
                                mm_done[ft] = t_mm
                            ncv.wait_ge(k.te.h, t_mm)
                            if qc == 0 and (ft, it) != (0, 0) and (ft, it) != (0, 1):
                                pft = (ft, it)
                                prow = row_t.get((ft - 1, it), 0)
                                ncv.wait_ge(k.sA.h, prow)
                            sc2 = QK_SCALE if ft < 16 else 1.0
                            ev_done[gi] = k.ve.inc(ncv.tensor_scalar(
                                ev_it[:, qc * 512:(qc + 1) * 512], pb,
                                bcols[:, ft:ft + 1], sc2, ALU.add, ALU.mult))
                            gi += 1
                        sync.wait_ge(k.ve.h, ev_done[gi - 1])
                        row_t[(ft, it)] = dA(qkvT[it, ft * 128:(ft + 1) * 128, :],
                                             ev_it[:, :])
                t_qkvT = k.sA.n

                for it in range(2):
                    for ts in range(16):
                        pp_step(
                            (k.sA.h, t_qkvT),
                            lambda st, it=it, ts=ts: dB(
                                st[:, :], qkvT[it, 2 * D:3 * D,
                                               ts * 128:(ts + 1) * 128],
                                transpose=True),
                            lambda st, it=it, ts=ts: dA(
                                vU[it, ts * 128:(ts + 1) * 128, :], st[:, :]))
                t_vU = k.sA.n
                t_a2_entry = (k.te.n, k.ve.n)

                ncv.wait_ge(k.sA.h, t_vU)
                t_ones = k.ve.inc(ncv.memset(vp2[0][:, :, 128:132], 1.0))
                t_ones = k.ve.inc(ncv.memset(vp2[1][:, :, 128:132], 1.0))
                ncv.wait_ge(k.ve.h, t_ones)
                oh_store = 0
                head_done = {}
                prev_exp = None
                prev_pv = None
                for it in range(2):
                    for h in range(NH):
                        hh = it * NH + h
                        par = hh % 2
                        kth, qth, vp = kth2[par], qth2[par], vp2[par]
                        if hh >= 2:
                            sync.wait_ge(k.te.h, head_done[hh - 2][0])
                            sync.wait_ge(k.ve.h, head_done[hh - 2][1])
                        else:
                            sync.wait_ge(k.te.h, t_a2_entry[0])
                            sync.wait_ge(k.ve.h, t_a2_entry[1])
                        sync.wait_ge(k.sA.h, t_vU)
                        dA(kth[:, :], qkvT[it, D + h * 128:D + (h + 1) * 128, :])
                        dA(qth[:, :], qkvT[it, h * 128:(h + 1) * 128, :])
                        dA(vp[:, :, 0:128],
                           vU[it, :, h * 128:(h + 1) * 128].rearrange(
                               "(kt p) e -> p kt e", p=128))
                        t_ld = k.sA.n
                        nct.wait_ge(k.sA.h, t_ld)
                        exp_done = {}
                        pv_ev = {}
                        pts_flat = big[:, 32768:40960]
                        for qc in range(4):
                            t_exp_last = 0
                            for g in range(4):
                                gg = qc * 4 + g
                                gpar = gg % 2
                                base = gpar * 2048
                                if gg >= 2:
                                    nct.wait_ge(k.se.h, exp_done[gg - 2])
                                elif prev_exp is not None:
                                    nct.wait_ge(k.se.h, prev_exp[14 + gg])
                                if gpar == 1 and qc > 0 and g == 1:
                                    nct.wait_ge(k.ve.h, pv_ev[(qc - 1) * 4 + 3])
                                elif gpar == 1 and qc == 0 and g == 1 and prev_pv is not None:
                                    nct.wait_ge(k.ve.h, prev_pv[15])
                                for kt4 in range(4):
                                    kt = g * 4 + kt4
                                    ins = nct.matmul(
                                        ps[:, base + kt4 * 512:base + (kt4 + 1) * 512],
                                        kth[:, kt * 128:(kt + 1) * 128],
                                        qth[:, qc * 512:(qc + 1) * 512],
                                        start=True, stop=True)
                                t_mm = k.te.inc(ins)
                                ncs.wait_ge(k.te.h, t_mm)
                                t_exp_last = k.se.inc(ncs.activation(
                                    pts_flat[:, g * 2048:(g + 1) * 2048],
                                    ps[:, base:base + 2048], AF.Exp, bias=f32(-SHIFT)))
                                exp_done[gg] = t_exp_last
                            nct.wait_ge(k.se.h, t_exp_last)
                            for qt in range(4):
                                gq = qc * 4 + qt
                                po = ps[:, 2048 + (gq % 2) * 512:2048 + (gq % 2) * 512 + 132]
                                if gq >= 2:
                                    nct.wait_ge(k.ve.h, pv_ev[gq - 2])
                                elif prev_pv is not None:
                                    nct.wait_ge(k.ve.h, prev_pv[14 + gq])
                                if gq == 0:
                                    ncv.wait_ge(k.sA.h, oh_store)
                                for kt in range(16):
                                    ins = nct.matmul(po[:, 0:129],
                                                     pts[:, kt, qt * 128:(qt + 1) * 128],
                                                     vp[:, kt, 0:129],
                                                     start=(kt == 0), stop=(kt == 15))
                                t_pv = k.te.inc(ins)
                                ncv.wait_ge(k.te.h, t_pv)
                                rc = recb[:, gq % 2:gq % 2 + 1]
                                k.ve.inc(ncv.reciprocal(rc, po[:, 128:129]))
                                ncv.wait_ge(k.ve.h, k.ve.n)
                                pv_ev[gq] = k.ve.inc(ncv.tensor_scalar(
                                    oh[:, gq, :], po[:, 0:128], rc, None, ALU.mult))
                        sync.wait_ge(k.ve.h, pv_ev[15])
                        oh_store = dA(
                            attn[it, :, h * 128:(h + 1) * 128].rearrange(
                                "(j p) e -> p j e", p=128),
                            oh[:, :, :])
                        head_done[hh] = (k.te.n, k.ve.n)
                        prev_exp = exp_done
                        prev_pv = pv_ev
                t_attn = k.sA.n

                nc.gpsimd.wait_ge(k.sA.h, t_attn)
                nc.gpsimd.wait_ge(k.sW.h, t_wouta_cast)
                for ds in range(0, 16, 4):
                    dW(wouta_sb[:, ds:ds + 4, :],
                       wouta_bf[ds * 128:(ds + 4) * 128, :].rearrange(
                           "(q p) e -> p q e", p=128))
                t_wo = k.sW.n
                state = {}
                aoT_t = 0
                for it in range(2):
                    tok0 = it * LI
                    nc.scalar.wait_ge(k.te.h, k.te.n)
                    for ds in range(16):
                        nc.scalar.wait_ge(k.sA.h, t_attn)
                        dB(aoT[:, ds, :], attn[it, :, ds * 128:(ds + 1) * 128],
                           transpose=True)
                    t_aoT = k.sB.n
                    nct.wait_ge(k.sW.h, t_wo)
                    nct.wait_ge(k.sB.h, t_aoT)
                    nct.wait_ge(k.ve.h, k.ve.n)
                    ev_t = {}
                    for tt in range(16):
                        par = tt % 2
                        pY = ps[:, par * 2048:par * 2048 + 2048]
                        if tt >= 2:
                            nct.wait_ge(k.ve.h, ev_t[tt - 2])
                        for ds in range(16):
                            for c in range(4):
                                nct.matmul(pY[:, c * 512:(c + 1) * 512],
                                           aoT[:, ds, tt * 128:(tt + 1) * 128],
                                           wouta_sb[:, ds, c * 512:(c + 1) * 512],
                                           start=(ds == 0), stop=False)
                        for c in range(4):
                            ins = nct.matmul(pY[:, c * 512:(c + 1) * 512],
                                             ones_bf[0:1, :],
                                             brow_bf[0:1, c * 512:(c + 1) * 512],
                                             start=False, stop=True)
                        t_mm = k.te.inc(ins)
                        ncv.wait_ge(k.te.h, t_mm)
                        ev_t[tt] = residual_evac_tile(pY, xsrc, xdst,
                                                      tok0 + tt * 128,
                                                      par, None, state, do_ln=True)
                return k.sA.n

            def ffn(xsrc, xdst, xn_ready, emit_ln=False):
                chunks = []
                t0 = 0
                while t0 < NTOK:
                    cl = min(CH, NTOK - t0)
                    chunks.append((t0, cl))
                    t0 += cl
                t_gate_prev = 0
                dn_ev_last = 0
                ln_pending = None
                for ci, (c0, cl) in enumerate(chunks):
                    ntt = cl // 128
                    nc.scalar.wait_ge(k.te.h, t_gate_prev)
                    t_xnc = load_xnT(xnT_c, c0, cl, xn_ready)
                    nct.wait_ge(k.sB.h, t_xnc)
                    if ci > 0:
                        nct.wait_ge(k.ve.h, dn_ev_last)
                    ght = {}
                    for ht in range(64):
                        par = ht % 2
                        if ht >= 2:
                            nc.gpsimd.wait_ge(k.te.h, ght[ht - 2][0])
                        else:
                            nc.gpsimd.wait_ge(k.te.h, t_gate_prev)
                        t_w = dW(wgu[par], wguT[ht].rearrange("g ds p e -> p g ds e"))
                        pg = ps[:, par * 1024:par * 1024 + cl]
                        pu = ps[:, 2048 + par * 1024:2048 + par * 1024 + cl]
                        nct.wait_ge(k.sW.h, t_w)
                        if ht >= 2:
                            nct.wait_ge(k.ve.h, ght[ht - 2][1])
                        nsplit = [(0, 512), (512, cl)] if cl > 512 else [(0, cl)]
                        for ds in range(16):
                            for (a, b) in nsplit:
                                ins = nct.matmul(pg[:, a:b],
                                                 wgu[par][:, 0, ds, :],
                                                 xnT_c[:, ds, a:b],
                                                 start=(ds == 0),
                                                 stop=(ds == 15 and b == cl))
                        t_g = k.te.inc(ins)
                        for ds in range(16):
                            for (a, b) in nsplit:
                                ins = nct.matmul(pu[:, a:b],
                                                 wgu[par][:, 1, ds, :],
                                                 xnT_c[:, ds, a:b],
                                                 start=(ds == 0),
                                                 stop=(ds == 15 and b == cl))
                        t_u = k.te.inc(ins)
                        ncs.wait_ge(k.te.h, t_g)
                        if ht >= 1:
                            ncs.wait_ge(k.ve.h, ght[ht - 1][1])
                        t_si = k.se.inc(ncs.activation(silu_sb[:, 0:cl], pg, AF.Silu))
                        ncv.wait_ge(k.se.h, t_si)
                        ncv.wait_ge(k.te.h, t_u)
                        t_gt = k.ve.inc(ncv.tensor_tensor(gTf[:, ht, 0:cl],
                                                          silu_sb[:, 0:cl], pu,
                                                          ALU.mult))
                        ght[ht] = (t_u, t_gt)
                    t_gate_prev = ght[63][0]
                    nct.wait_ge(k.ve.h, ght[63][1])
                    nct.wait_ge(k.ve.h, ght[62][1])
                    dn_mm = {}
                    dn_ev = {}
                    slice_store = {}
                    slice_ev = {}
                    for qd in range(4):
                        for hsg in range(16):
                            par = hsg % 2
                            gidx = qd * 16 + hsg
                            if gidx >= 2:
                                nc.gpsimd.wait_ge(k.te.h, dn_mm[gidx - 2])
                            else:
                                nc.gpsimd.wait_ge(k.te.h, ght[63][0])
                            t_wd = dW(wdt[par],
                                      wdT[hsg * 512:(hsg + 1) * 512,
                                          qd * 512:(qd + 1) * 512].rearrange(
                                              "(j p) e -> p j e", p=128))
                            nct.wait_ge(k.sW.h, t_wd)
                            if hsg == 0 and qd > 0:
                                nct.wait_ge(k.ve.h, dn_ev[(qd - 1, ntt - 1)])
                            for j in range(4):
                                hs = hsg * 4 + j
                                for tt in range(ntt):
                                    ins = nct.matmul(ps[:, tt * 512:(tt + 1) * 512],
                                                     gTf[:, hs, tt * 128:(tt + 1) * 128],
                                                     wdt[par][:, j, :],
                                                     start=(hs == 0), stop=(hs == 63))
                            dn_mm[gidx] = k.te.inc(ins)
                        t_q = dn_mm[qd * 16 + 15]
                        for tt in range(ntt):
                            r0 = c0 + tt * 128
                            sl = (tt % 2, qd % 2)
                            xt = xf32[:, (tt % 2) * D + (qd % 2) * 512:
                                      (tt % 2) * D + (qd % 2) * 512 + 512]
                            if sl in slice_ev:
                                nc.scalar.wait_ge(k.ve.h, slice_ev[sl])
                            t_x = dB(xt, xsrc[r0:r0 + 128, qd * 512:(qd + 1) * 512])
                            ncv.wait_ge(k.sB.h, t_x)
                            ncv.wait_ge(k.te.h, t_q)
                            if (qd % 2,) in slice_store:
                                ncv.wait_ge(k.sA.h, slice_store[(qd % 2,)])
                            ev = evf32[:, (qd % 2) * 512:(qd % 2) * 512 + 512]
                            t_ev = k.ve.inc(ncv.tensor_tensor(
                                ev, ps[:, tt * 512:(tt + 1) * 512], xt, ALU.add))
                            dn_ev[(qd, tt)] = t_ev
                            slice_ev[sl] = t_ev
                            sync.wait_ge(k.ve.h, t_ev)
                            dA(xdst[r0:r0 + 128, qd * 512:(qd + 1) * 512], ev)
                            slice_store[(qd % 2,)] = k.sA.n
                    dn_ev_last = dn_ev[(3, ntt - 1)]
                return k.sA.n

            def attn_b(item, xsrc, xdst, xn_ready, stop=None):
                tok0 = item * LI
                t_xnT = load_xnT(xnT, tok0, LI, xn_ready)
                nct.wait_ge(k.sB.h, t_xnT)

                mm_done = {}
                ev_done = {}
                for ft in range(64):
                    par = ft % 2
                    wt = (wt0, wt1)[par]
                    if ft >= 2:
                        nc.gpsimd.wait_ge(k.te.h, mm_done[ft - 2])
                    t_w = dW(wt.rearrange("p (ds e) -> p ds e", e=128),
                             wqkvuT[ft].rearrange("ds p e -> p ds e"))
                    nct.wait_ge(k.sW.h, t_w)
                    for qc in range(4):
                        gi = ft * 4 + qc
                        pb = ps[:, (gi % 2) * 512:(gi % 2) * 512 + 512]
                        if gi >= 2:
                            nct.wait_ge(k.se.h, ev_done[gi - 2])
                        for ds in range(16):
                            ins = nct.matmul(pb, wt[:, ds * 128:(ds + 1) * 128],
                                             xnT[:, ds, qc * 512:(qc + 1) * 512],
                                             start=(ds == 0), stop=(ds == 15))
                        t_mm = k.te.inc(ins)
                        if qc == 3:
                            mm_done[ft] = t_mm
                        ncs.wait_ge(k.te.h, t_mm)
                        func = AF.Silu if ft < 32 else AF.Copy
                        ev_done[gi] = k.se.inc(ncs.activation(
                            evst[:, qc * 512:(qc + 1) * 512], pb, func))
                    sync.wait_ge(k.se.h, ev_done[ft * 4 + 3])
                    t_row = dA(qkvuTb[ft * 128:(ft + 1) * 128, :], evst[:, :])
                    ncs.wait_ge(k.sA.h, t_row)
                t_qkvuT = k.sA.n
                if stop == "b2a":
                    barrier()
                    return k.sA.n

                for j, (src_r, dst) in enumerate(((D, kUb), (2 * D, vUb))):
                    for ts in range(16):
                        pp_step(
                            (k.sA.h, t_qkvuT),
                            lambda st, src_r=src_r, ts=ts: dB(
                                st[:, :], qkvuTb[src_r:src_r + D,
                                                 ts * 128:(ts + 1) * 128],
                                transpose=True),
                            lambda st, dst=dst, ts=ts: dA(
                                dst[ts * 128:(ts + 1) * 128, :], st[:, :]))
                t_kvU = k.sA.n
                if stop == "b2a5":
                    barrier()
                    return k.sA.n

                psO = ps[:, 0:256]
                psQK = (ps[:, 512:768], ps[:, 768:1024])
                psRC = ps[:, 1024:1040]
                psKV = ps[:, 1536:1664]
                for h in range(SCAN_NH):
                    sync.wait_ge(k.te.h, k.te.n)
                    sync.wait_ge(k.ve.h, k.ve.n)
                    sync.wait_ge(k.sA.h, t_kvU)
                    dA(qtb[:, :], qkvuTb[h * 128:(h + 1) * 128, :])
                    dA(ktb[:, :], qkvuTb[D + h * 128:D + (h + 1) * 128, :])
                    dA(utb[:, :], qkvuTb[3 * D + h * 128:3 * D + (h + 1) * 128, :])
                    dA(ks3[:, :, :], kUb[:, h * 128:(h + 1) * 128].rearrange(
                        "(j p) e -> p j e", p=128))
                    dA(vs3[:, :, :], vUb[:, h * 128:(h + 1) * 128].rearrange(
                        "(j p) e -> p j e", p=128))
                    t_ld = k.sA.n
                    scol = slope_sb[:, h:h + 1]
                    ncv.wait_ge(k.te.h, k.te.n)
                    ncv.wait_ge(k.ve.h, k.ve.n)
                    t1 = k.ve.inc(ncv.tensor_scalar(dec_f[:, :], iota_sb[:, :],
                                                    scol, None, ALU.mult))
                    ncs.wait_ge(k.ve.h, t1)
                    ncs.wait_ge(k.se.h, k.se.n)
                    tq = k.se.inc(ncs.activation(qdec[:, :], dec_f[:, :], AF.Exp,
                                                 scale=f32(-1.0)))
                    for st in range(2):
                        ncv.wait_ge(k.se.h, k.se.n)
                        t2 = k.ve.inc(ncv.tensor_scalar(dec_f[:, :],
                                                        idxt_sb[:, st, :],
                                                        scol, None, ALU.mult))
                        ncs.wait_ge(k.ve.h, t2)
                        t3 = k.se.inc(ncs.activation(dtmp[:, :], dec_f[:, :],
                                                     AF.Exp, scale=f32(-1.0)))
                        ncv.wait_ge(k.se.h, t3)
                        k.ve.inc(ncv.tensor_tensor(ddm[:, st, :], dtmp[:, :],
                                                   maskt_sb[:, st, :], ALU.mult))
                    ncv.wait_ge(k.se.h, k.se.n)
                    t4 = k.ve.inc(ncv.tensor_scalar(dec_f[:, 0:2], iotc_sb[:, :],
                                                    256.0, scol,
                                                    ALU.subtract, ALU.mult))
                    ncs.wait_ge(k.ve.h, t4)
                    t5 = k.se.inc(ncs.activation(kdec[:, :], dec_f[:, 0:2], AF.Exp))
                    t5b = k.se.inc(ncs.activation(bdcol[:, :], scol, AF.Exp,
                                                  scale=f32(-256.0)))
                    ncv.wait_ge(k.se.h, t5b)
                    k.ve.inc(ncv.memset(kv_sb[:, :], 0.0))
                    t_kvbf = k.ve.inc(ncv.memset(kv_bf[:, :], 0.0))
                    nct.wait_ge(k.sA.h, t_ld)
                    if h == 0:
                        nct.wait_ge(k.se.h, k.se.n)
                    t_a = k.ve.n
                    t_sq = k.ve.n
                    t_gt = k.ve.n
                    t_cp_prev = k.se.n
                    for blk in range(SCAN_NB):
                        m0 = blk * BLK
                        nct.wait_ge(k.ve.h, t_a)
                        for st in range(2):
                            t_qk = k.te.inc(nct.matmul(
                                psQK[st], ktb[:, m0 + st * 128:m0 + st * 128 + 128],
                                qtb[:, m0:m0 + BLK], start=True, stop=True))
                        ncv.wait_ge(k.te.h, t_qk)
                        tqs = k.ve.inc(ncv.tensor_tensor(qsc[:, :],
                                                         qtb[:, m0:m0 + BLK],
                                                         qdec[:, :], ALU.mult))
                        for st in range(2):
                            t_a = k.ve.inc(ncv.tensor_tensor(asb[:, st, :],
                                                             psQK[st],
                                                             ddm[:, st, :],
                                                             ALU.mult))
                        for st in range(2):
                            t_kt = k.ve.inc(ncv.tensor_scalar(
                                ksc[:, st, :], ks3[:, 2 * blk + st, :],
                                kdec[:, st:st + 1], None, ALU.mult))
                        nct.wait_ge(k.ve.h, tqs)
                        nct.wait_ge(k.ve.h, t_kvbf)
                        nct.wait_ge(k.se.h, t_cp_prev)
                        nct.matmul(psO, kv_bf[:, :], qsc[:, :], start=True, stop=False)
                        nct.wait_ge(k.ve.h, t_a)
                        for st in range(2):
                            ins = nct.matmul(psO, vs3[:, 2 * blk + st, :],
                                             asb[:, st, :],
                                             start=False, stop=(st == 1))
                        t_o = k.te.inc(ins)
                        ncs.wait_ge(k.te.h, t_o)
                        ncs.wait_ge(k.ve.h, t_gt)
                        t_cp = k.se.inc(ncs.activation(o_sb[:, :], psO, AF.Copy))
                        ncv.wait_ge(k.se.h, t_cp)
                        t_gt = k.ve.inc(ncv.tensor_tensor(gTb[:, h, m0:m0 + BLK],
                                                          o_sb[:, :],
                                                          utb[:, m0:m0 + BLK],
                                                          ALU.mult))
                        t_sq = k.ve.inc(ncv.tensor_tensor(sq_bf[:, :], o_sb[:, :],
                                                          o_sb[:, :], ALU.mult))
                        nct.wait_ge(k.ve.h, t_sq)
                        for st in range(2):
                            c = 2 * blk + st
                            nct.matmul(psRC[:, c:c + 1],
                                       sq_bf[:, st * 128:st * 128 + 128],
                                       ones_col[:, 0:1],
                                       start=(h == 0 and blk == 0 and st == 0),
                                       stop=(h == SCAN_NH - 1 and blk == SCAN_NB - 1 and st == 1))
                        nct.wait_ge(k.ve.h, t_kt)
                        for st in range(2):
                            ins = nct.matmul(psKV, ksc[:, st, :],
                                             vs3[:, 2 * blk + st, :],
                                             start=(st == 0), stop=(st == 1))
                        t_kv = k.te.inc(ins)
                        ncv.wait_ge(k.te.h, t_kv)
                        k.ve.inc(ncv.scalar_tensor_tensor(kv_sb[:, :], kv_sb[:, :],
                                                          bdcol[:, 0:1], psKV,
                                                          ALU.mult, ALU.add))
                        ncv.wait_ge(k.ve.h, k.ve.n)
                        t_kvbf = k.ve.inc(ncv.tensor_copy(kv_bf[:, :], kv_sb[:, :]))
                        t_cp_prev = t_cp
                t_rc_done = k.te.n
                if stop == "scan":
                    barrier()
                    return k.sA.n

                ncs.wait_ge(k.te.h, t_rc_done)
                ncs.wait_ge(k.se.h, k.se.n)
                t_l = k.se.inc(ncs.activation(lnc[:, :], psRC, AF.Ln,
                                              bias=f32(EPS_RMS), scale=f32(1.0 / D)))
                ncs.wait_ge(k.se.h, t_l)
                t_rsc = k.se.inc(ncs.activation(rstdc[:, :], lnc[:, :], AF.Exp,
                                                scale=f32(-0.5)))

                nc.scalar.wait_ge(k.ve.h, k.ve.n)
                for hs in range(16):
                    dB(woutbT[:, hs, :], woutb_bf[:, hs * 128:(hs + 1) * 128],
                       transpose=True)
                t_wob = k.sB.n
                nct.wait_ge(k.sB.h, t_wob)
                nct.wait_ge(k.ve.h, k.ve.n)
                nct.wait_ge(k.se.h, t_rsc)
                ev_t = {}
                state = {}
                for tt in range(16):
                    par = tt % 2
                    pY = ps[:, par * 2048:par * 2048 + 2048]
                    if tt >= 2:
                        nct.wait_ge(k.ve.h, ev_t[tt - 2])
                    for hs in range(16):
                        for c in range(4):
                            ins = nct.matmul(pY[:, c * 512:(c + 1) * 512],
                                             gTb[:, hs, tt * 128:(tt + 1) * 128],
                                             woutbT[:, hs, c * 512:(c + 1) * 512],
                                             start=(hs == 0), stop=(hs == 15))
                    t_mm = k.te.inc(ins)
                    ncv.wait_ge(k.te.h, t_mm)
                    ncv.wait_ge(k.se.h, t_rsc)
                    ev_t[tt] = residual_evac_tile(pY, xsrc, xdst, tok0 + tt * 128,
                                                  par, rstdc[:, tt:tt + 1],
                                                  state, do_ln=True)
                return k.sA.n

            def stop(name):
                return stop_after == name

            while True:
                if stop("prepass"):
                    barrier()
                    break
                t_ln0 = ln_range(x_in, 0, NTI)
                t_ln1 = ln_range(x_in, LI, NTI)
                soft()
                if stop("ln0"):
                    barrier()
                    break
                attn_a_all(x_in, xs1, k.sA.n)
                while bgc:
                    bgc.popleft()()
                t_prep = prep_all()
                soft()
                if stop("attn_a1") or stop("attn_a0"):
                    barrier()
                    break
                nc.gpsimd.wait_ge(k.sA.h, t_prep)
                ffn(xs1, xs2, k.sA.n)
                soft()
                if stop("ffn1"):
                    barrier()
                    break
                t_ln2 = ln_range(xs2, 0, NTT, x_ready=k.sA.n)
                soft()
                if stop("ln2"):
                    barrier()
                    break
                if stop_after in ("b2a", "b2a5", "scan"):
                    attn_b(0, xs2, xs3, t_ln2)
                    break
                attn_b(0, xs2, xs3, t_ln2)
                soft()
                if stop("attn_b0"):
                    barrier()
                    break
                attn_b(1, xs2, xs3, t_ln2)
                soft()
                if stop("attn_b1"):
                    barrier()
                    break
                ffn(xs3, y, k.sA.n)
                barrier()
                break

    ctx.close()
    return nc


def kernel(L0_attn_a_x, L0_attn_a_w_qkv, L0_attn_a_b_qkv, L0_attn_a_w_out,
           L0_attn_a_b_out, L0_attn_b_slope_rate, L0_attn_b_W_qkvu,
           L0_attn_b_W_out, L0_ffn_w_gate, L0_ffn_w_up, L0_ffn_w_down):
    x = np.ascontiguousarray(np.asarray(L0_attn_a_x, dtype=np.float32))
    wqkv = np.ascontiguousarray(np.asarray(L0_attn_a_w_qkv, dtype=np.float32))
    bqkv = np.ascontiguousarray(np.asarray(L0_attn_a_b_qkv, dtype=np.float32))
    wouta_ = np.ascontiguousarray(np.asarray(L0_attn_a_w_out, dtype=np.float32))
    bouta = np.ascontiguousarray(np.asarray(L0_attn_a_b_out, dtype=np.float32))
    slope_ = np.ascontiguousarray(np.asarray(L0_attn_b_slope_rate, dtype=np.float32))
    wqkvu_ = np.ascontiguousarray(np.asarray(L0_attn_b_W_qkvu, dtype=np.float32))
    woutb_ = np.ascontiguousarray(np.asarray(L0_attn_b_W_out, dtype=np.float32))
    wgate = np.ascontiguousarray(np.asarray(L0_ffn_w_gate, dtype=np.float32))
    wup = np.ascontiguousarray(np.asarray(L0_ffn_w_up, dtype=np.float32))
    wdown = np.ascontiguousarray(np.asarray(L0_ffn_w_down, dtype=np.float32))

    bqkv_cols = np.ascontiguousarray(bqkv.reshape(48, 128).T)
    bout_row = np.ascontiguousarray(bouta.reshape(1, D))
    slope_b = np.ascontiguousarray(np.broadcast_to(slope_.reshape(1, NH), (128, NH)))
    ar = np.arange(1, BLK + 1, dtype=np.float32)
    iota_row = np.ascontiguousarray(np.broadcast_to(ar[None, :], (128, BLK)))
    iota_col2 = np.ascontiguousarray(ar.reshape(2, 128).T)
    sm = ar[None, :] - ar[:, None]
    idxt = np.ascontiguousarray(np.maximum(sm, 0.0).astype(np.float32))
    maskt = np.ascontiguousarray((sm >= 0).astype(np.float32))

    nc = build()
    in_maps = []
    for c in range(8):
        xi = np.ascontiguousarray(x[2 * c:2 * c + 2].reshape(NTOK, D))
        in_maps.append(dict(
            x=xi, wqkv=wqkv, bqkv_cols=bqkv_cols,
            bout_row=bout_row, wouta=wouta_, wqkvu=wqkvu_, woutb=woutb_,
            slope=slope_b, wg=wgate, wu=wup, wd=wdown, iota_row=iota_row,
            iota_col2=iota_col2, idxt=idxt, maskt=maskt))
    res = run_bass_kernel_spmd(nc, in_maps, core_ids=list(range(8)))
    out = np.empty((16, LI, D), dtype=np.float32)
    for c in range(8):
        out[2 * c:2 * c + 2] = res.results[c]["y"].reshape(2, LI, D)
    return out


if __name__ == "__main__":
    import reference as R
    d = R.setup_inputs(0)
    inputs = {kk: np.asarray(v) for kk, v in d.items()}
    out = kernel(**inputs)
    ref = np.asarray(R.reference(**d))
    num = float(((out - ref) ** 2).mean())
    den = float((ref ** 2).mean())
    print("resvar:", num / den)


# version 12
# speedup vs baseline: 1.1504x; 1.0349x over previous
import collections
import contextlib

import numpy as np
import concourse.bass as bass
import concourse.mybir as mybir
from concourse.bass_utils import run_bass_kernel_spmd

F32 = mybir.dt.float32
BF16 = mybir.dt.bfloat16
AF = mybir.ActivationFunctionType
ALU = mybir.AluOpType

D = 2048
LI = 2048
ITEMS = 2
NTOK = LI * ITEMS
NTI = LI // 128
NTT = NTOK // 128
HF = 8192
NH = 16
HD = 128
BLK = 256
NB = LI // BLK
CH = 768
SHIFT = 12.0
QK_SCALE = float(1.0 / np.sqrt(np.float32(128.0)))
EPS_LN = 1e-5
EPS_RMS = 1e-6
f32 = np.float32
SCAN_NH = NH
SCAN_NB = NB


class Sem:
    def __init__(self, ctx, nc, name):
        self.h = ctx.enter_context(nc.semaphore(name))
        self.n = 0

    def inc(self, instr, k=1):
        instr.then_inc(self.h, k)
        self.n += k
        return self.n


def build(debug_names=(), stop_after=None):
    nc = bass.Bass("TRN2", target_bir_lowering=False, debug=False)
    ctx = contextlib.ExitStack()

    def dram(name, shape, dtype, kind):
        if kind == "Internal" and name in debug_names:
            kind = "ExternalOutput"
        return nc.dram_tensor(name, shape, dtype, kind=kind).ap()

    x_in = dram("x", [NTOK, D], F32, "ExternalInput")
    wqkv = dram("wqkv", [D, 3 * D], F32, "ExternalInput")
    bqkv_cols = dram("bqkv_cols", [128, 48], F32, "ExternalInput")
    bout_row = dram("bout_row", [1, D], F32, "ExternalInput")
    wouta = dram("wouta", [D, D], F32, "ExternalInput")
    wqkvu = dram("wqkvu", [HF, D], F32, "ExternalInput")
    woutb = dram("woutb", [D, D], F32, "ExternalInput")
    slope = dram("slope", [128, NH], F32, "ExternalInput")
    wg = dram("wg", [HF, D], F32, "ExternalInput")
    wu = dram("wu", [HF, D], F32, "ExternalInput")
    wd = dram("wd", [D, HF], F32, "ExternalInput")
    iota_row = dram("iota_row", [128, BLK], F32, "ExternalInput")
    iota_col2 = dram("iota_col2", [128, 2], F32, "ExternalInput")
    idxt = dram("idxt", [BLK, BLK], F32, "ExternalInput")
    maskt = dram("maskt", [BLK, BLK], F32, "ExternalInput")
    y = dram("y", [NTOK, D], F32, "ExternalOutput")

    xs1 = dram("xs1", [NTOK, D], F32, "Internal")
    xs2 = dram("xs2", [NTOK, D], F32, "Internal")
    xs3 = dram("xs3", [NTOK, D], F32, "Internal")
    xnscr = dram("xnscr", [NTOK, D], BF16, "Internal")
    wqkv_bf = dram("wqkv_bf", [D, 3 * D], BF16, "Internal")
    wouta_bf = dram("wouta_bf", [D, D], BF16, "Internal")
    woutb_bf = dram("woutb_bf", [D, D], BF16, "Internal")
    wqkvu_bf = dram("wqkvu_bf", [HF, D], BF16, "Internal")
    wg_bf = dram("wg_bf", [HF, D], BF16, "Internal")
    wu_bf = dram("wu_bf", [HF, D], BF16, "Internal")
    wd_bf = dram("wd_bf", [D, HF], BF16, "Internal")
    wqkvuT = dram("wqkvuT", [64, 16, 128, 128], BF16, "Internal")
    wguT = dram("wguT", [64, 2, 16, 128, 128], BF16, "Internal")
    wdT = dram("wdT", [HF, D], BF16, "Internal")
    qkvT = dram("qkvT", [2, 3 * D, LI], BF16, "Internal")
    vU = dram("vU", [2, LI, D], BF16, "Internal")
    attn = dram("attn", [2, LI, D], BF16, "Internal")
    qkvuTb = dram("qkvuTb", [4 * D, LI], BF16, "Internal")
    kUb = dram("kUb", [LI, D], BF16, "Internal")
    vUb = dram("vUb", [LI, D], BF16, "Internal")

    BIGC = 65536
    big = nc.alloc_sbuf_tensor("big", [128, BIGC], BF16)
    SMC = 14464
    small = nc.alloc_sbuf_tensor("small", [128, SMC], BF16)
    xf32 = nc.alloc_sbuf_tensor("xf32", [128, 2 * D], F32)
    evf32 = nc.alloc_sbuf_tensor("evf32", [128, D], F32)
    xnorm = nc.alloc_sbuf_tensor("xnorm", [128, D], BF16)
    bn24 = nc.alloc_sbuf_tensor("bn24", [128, 24], F32)
    mvb = nc.alloc_sbuf_tensor("mvb", [128, 4], F32)
    lnvb = nc.alloc_sbuf_tensor("lnvb", [128, 2], F32)
    rstdb = nc.alloc_sbuf_tensor("rstdb", [128, 2], F32)
    bcols = nc.alloc_sbuf_tensor("bcols", [128, 48], F32)
    brow_bf = nc.alloc_sbuf_tensor("brow_bf", [1, D], BF16)
    ones_bf = nc.alloc_sbuf_tensor("ones_bf", [1, 128], BF16)
    ones_col = nc.alloc_sbuf_tensor("ones_col", [128, 1], BF16)
    slope_sb = nc.alloc_sbuf_tensor("slope_sb", [128, NH], F32)
    iota_sb = nc.alloc_sbuf_tensor("iota_sb", [128, BLK], F32)
    iotc_sb = nc.alloc_sbuf_tensor("iotc_sb", [128, 2], F32)
    idxt_sb = nc.alloc_sbuf_tensor("idxt_sb", [128, 2, BLK], BF16)
    maskt_sb = nc.alloc_sbuf_tensor("maskt_sb", [128, 2, BLK], BF16)
    rstdc = nc.alloc_sbuf_tensor("rstdc", [128, NTI], F32)
    lnc = nc.alloc_sbuf_tensor("lnc", [128, NTI], F32)
    kv_sb = nc.alloc_sbuf_tensor("kv_sb", [128, 128], F32)
    kv_bf = nc.alloc_sbuf_tensor("kv_bf", [128, 128], BF16)
    sq_bf = nc.alloc_sbuf_tensor("sq_bf", [128, BLK], BF16)
    o_sb = nc.alloc_sbuf_tensor("o_sb", [128, BLK], BF16)
    dec_f = nc.alloc_sbuf_tensor("dec_f", [128, BLK], F32)
    qdec = nc.alloc_sbuf_tensor("qdec", [128, BLK], BF16)
    ddm = nc.alloc_sbuf_tensor("ddm", [128, 2, BLK], BF16)
    dtmp = nc.alloc_sbuf_tensor("dtmp", [128, BLK], BF16)
    kdec = nc.alloc_sbuf_tensor("kdec", [128, 2], F32)
    bdcol = nc.alloc_sbuf_tensor("bdcol", [128, 1], F32)
    ksc = nc.alloc_sbuf_tensor("ksc", [128, 2, 128], BF16)
    qsc = nc.alloc_sbuf_tensor("qsc", [128, BLK], BF16)
    asb = nc.alloc_sbuf_tensor("asb", [128, 2, BLK], BF16)
    recb = nc.alloc_sbuf_tensor("recb", [128, 2], F32)
    tstA = nc.alloc_sbuf_tensor("tstA", [128, 2048], BF16)
    tstB = nc.alloc_sbuf_tensor("tstB", [128, 2048], BF16)

    ps = ctx.enter_context(nc.psum_tensor("ps", [128, 4096], F32))

    k = K = type("KK", (), {})()
    k.sW = Sem(ctx, nc, "sW")
    k.sA = Sem(ctx, nc, "sA")
    k.sB = Sem(ctx, nc, "sB")
    k.te = Sem(ctx, nc, "te")
    k.ve = Sem(ctx, nc, "ve")
    k.se = Sem(ctx, nc, "se")

    def dW(out, in_, **kw):
        return k.sW.inc(nc.gpsimd.dma_start(out=out, in_=in_, **kw), 16)

    def dA(out, in_, **kw):
        return k.sA.inc(nc.sync.dma_start(out=out, in_=in_, **kw), 16)

    def dB(out, in_, **kw):
        return k.sB.inc(nc.scalar.dma_start(out=out, in_=in_, **kw), 16)

    nct = nc.tensor
    ncv = nc.vector
    ncs = nc.scalar
    bgc = collections.deque()

    def pump(n=1):
        for _ in range(n):
            if not bgc:
                return
            bgc.popleft()()

    def barrier():
        for eng in (nct, ncv, ncs, nc.gpsimd, nc.sync):
            for s in (k.sW, k.sA, k.sB, k.te, k.ve, k.se):
                eng.wait_ge(s.h, s.n)

    def soft():
        nct.wait_ge(k.ve.h, k.ve.n)
        nct.wait_ge(k.se.h, k.se.n)
        for eng in (nc.sync, nc.scalar, nc.gpsimd):
            eng.wait_ge(k.te.h, k.te.n)
            eng.wait_ge(k.ve.h, k.ve.n)
            eng.wait_ge(k.se.h, k.se.n)
        ncv.wait_ge(k.sA.h, k.sA.n)
        ncv.wait_ge(k.sB.h, k.sB.n)

    xnT = big[:, 0:32768].rearrange("p (ds t) -> p ds t", t=2048)
    xnT1 = big[:, 32768:65536].rearrange("p (ds t) -> p ds t", t=2048)
    pts = big[:, 32768:40960].rearrange("p (kt q) -> p kt q", q=512)
    stage0 = big[:, 40960:49152]
    stage1 = big[:, 49152:57344]
    wouta_sb = big[:, 0:32768].rearrange("p (ds e) -> p ds e", e=2048)
    aoT = big[:, 32768:65536].rearrange("p (ds t) -> p ds t", t=2048)
    gTf = big[:, 0:49152].rearrange("p (h t) -> p h t", t=CH)
    xnT_c = big[:, 49152:61440].rearrange("p (ds t) -> p ds t", t=CH)
    gTb = big[:, 0:32768].rearrange("p (h t) -> p h t", t=2048)
    woutbT = big[:, 32768:65536].rearrange("p (h e) -> p h e", e=2048)

    wt0 = small[:, 0:2048]
    wt1 = small[:, 2048:4096]
    evst = small[:, 4096:6144]
    evst1 = small[:, 6144:8192]
    tst0 = tstA[:, :]
    tst1 = tstB[:, :]
    kth2 = [small[:, 0:2048], small[:, 4096:6144]]
    qth2 = [small[:, 2048:4096], small[:, 6144:8192]]
    vp2 = [small[:, 8192:10304].rearrange("p (kt e) -> p kt e", e=132),
           small[:, 10304:12416].rearrange("p (kt e) -> p kt e", e=132)]
    oh = small[:, 12416:14464].rearrange("p (j e) -> p j e", e=128)
    wgu = [small[:, 0:4096].rearrange("p (g ds e) -> p g ds e", g=2, e=128),
           small[:, 4096:8192].rearrange("p (g ds e) -> p g ds e", g=2, e=128)]
    silu_sb = small[:, 8192:9088]
    wdt = [small[:, 9216:11264].rearrange("p (j e) -> p j e", e=512),
           small[:, 11264:13312].rearrange("p (j e) -> p j e", e=512)]
    qtb = small[:, 0:2048]
    ktb = small[:, 2048:4096]
    utb = small[:, 4096:6144]
    ks3 = small[:, 6144:8192].rearrange("p (j e) -> p j e", e=128)
    vs3 = small[:, 8192:10240].rearrange("p (j e) -> p j e", e=128)

    with nc.Block() as block:

        @block.sync
        def _(sync):
            dW(bcols[:, :], bqkv_cols[:, :])
            dW(brow_bf[0:1, :], bout_row[0:1, :])
            dW(slope_sb[:, :], slope[:, :])
            dW(iota_sb[:, :], iota_row[:, :])
            dW(iotc_sb[:, :], iota_col2[:, :])
            dW(idxt_sb[:, :, :], idxt.rearrange("(st p) m -> p st m", p=128))
            dW(maskt_sb[:, :, :], maskt.rearrange("(st p) m -> p st m", p=128))
            t_cst = k.sW.n
            ncv.wait_ge(k.sW.h, t_cst)
            ncv.memset(ones_bf[:, :], 1.0)
            ncv.memset(ones_col[:, :], 1.0)
            ncv.memset(ones_col[:, :], 1.0)

            def cast_w(dst, src, rows, blk_rows):
                for r0 in range(0, rows, blk_rows):
                    dW(dst[r0:r0 + blk_rows, :], src[r0:r0 + blk_rows, :])
                return k.sW.n

            t_wqkv_cast = cast_w(wqkv_bf, wqkv, D, 256)
            t_wouta_cast = cast_w(wouta_bf, wouta, D, 512)
            cast_t = {}

            def enq_cast(key, dst, src, rows, blk_rows):
                def mk(r0):
                    def go():
                        dW(dst[r0:r0 + blk_rows, :], src[r0:r0 + blk_rows, :])
                        cast_t[key] = k.sW.n
                    return go
                for r0 in range(0, rows, blk_rows):
                    bgc.append(mk(r0))

            enq_cast("wg", wg_bf, wg, HF, 512)
            enq_cast("wu", wu_bf, wu, HF, 512)
            enq_cast("wd", wd_bf, wd, D, 128)
            enq_cast("wqkvu", wqkvu_bf, wqkvu, HF, 512)
            enq_cast("woutb", woutb_bf, woutb, D, 512)

            store_t = {}
            pp_i = [0]

            def pp_step(extra, emit_tr, emit_st):
                i = pp_i[0]
                st = (tst0, tst1)[i % 2]
                if i >= 2:
                    nc.scalar.wait_ge(k.sA.h, store_t[i - 2])
                if extra is not None:
                    nc.scalar.wait_ge(extra[0], extra[1])
                emit_tr(st)
                sync.wait_ge(k.sB.h, k.sB.n)
                emit_st(st)
                store_t[i] = k.sA.n
                pp_i[0] = i + 1

            bgp = collections.deque()

            def pump_pp(n=1):
                for _ in range(n):
                    if not bgp:
                        return
                    bgp.popleft()()

            def prep_enqueue_ffn():
                for key, dst_pair in (("wg", 0), ("wu", 1)):
                    for ds in range(16):
                        for q in range(4):
                            def step(key=key, dp=dst_pair, ds=ds, q=q):
                                src = wg_bf if key == "wg" else wu_bf
                                pp_step(
                                    (k.sW.h, cast_t[key]),
                                    lambda st: dB(
                                        st[:, :],
                                        src[q * 2048:(q + 1) * 2048,
                                            ds * 128:(ds + 1) * 128], transpose=True),
                                    lambda st: dA(
                                        wguT[q * 16:(q + 1) * 16, dp, ds].rearrange(
                                            "ft p e -> p ft e"),
                                        st.rearrange("p (ft e) -> p ft e", e=128)))
                            bgp.append(step)
                for hs in range(64):
                    def step(hs=hs):
                        pp_step(
                            (k.sW.h, cast_t["wd"]),
                            lambda st: dB(st[:, 0:2048],
                                          wd_bf[:, hs * 128:(hs + 1) * 128],
                                          transpose=True),
                            lambda st: dA(wdT[hs * 128:(hs + 1) * 128, :],
                                          st[:, 0:2048]))
                    bgp.append(step)

            def prep_qkvu():
                for ds in range(16):
                    for q in range(4):
                        pp_step(
                            (k.sW.h, cast_t["wqkvu"]),
                            lambda st, ds=ds, q=q: dB(
                                st[:, :], wqkvu_bf[q * 2048:(q + 1) * 2048,
                                                   ds * 128:(ds + 1) * 128],
                                transpose=True),
                            lambda st, ds=ds, q=q: dA(
                                wqkvuT[q * 16:(q + 1) * 16, ds].rearrange(
                                    "ft p e -> p ft e"),
                                st.rearrange("p (ft e) -> p ft e", e=128)))
                return k.sA.n

            def ln_core(src_sb, dst_r0, par, eps):
                xnt = xnorm[:, :]
                mv = mvb[:, par * 2:par * 2 + 2]
                lnv = lnvb[:, par:par + 1]
                rst = rstdb[:, par:par + 1]
                ncv.wait_ge(k.ve.h, k.ve.n)
                for i in range(4):
                    ins = ncv.bn_stats(bn24[:, 6 * i:6 * i + 6],
                                       src_sb[:, 512 * i:512 * (i + 1)])
                k.ve.inc(ins)
                ncv.wait_ge(k.ve.h, k.ve.n)
                t_mv = k.ve.inc(ncv.bn_aggr(mv, bn24[:, :]))
                ncs.wait_ge(k.ve.h, t_mv)
                ncs.wait_ge(k.se.h, k.se.n)
                t_ln = k.se.inc(ncs.activation(lnv, mv[:, 1:2], AF.Ln, bias=f32(eps)))
                ncs.wait_ge(k.se.h, t_ln)
                t_rs = k.se.inc(ncs.activation(rst, lnv, AF.Exp, scale=f32(-0.5)))
                ncv.wait_ge(k.se.h, t_rs)
                t_xn = k.ve.inc(ncv.tensor_scalar(xnt, src_sb, mv[:, 0:1], rst,
                                                  ALU.subtract, ALU.mult))
                sync.wait_ge(k.ve.h, t_xn)
                t_st = dA(xnscr[dst_r0:dst_r0 + 128, :], xnt)
                return t_st, t_xn

            xn_store = {0: 0}

            def ln_range(src, tok0, ntiles, eps=EPS_LN, x_ready=0):
                nc.scalar.wait_ge(k.sA.h, x_ready)
                prev = {}
                for tt in range(ntiles):
                    par = tt % 2
                    r0 = tok0 + tt * 128
                    xt = xf32[:, par * D:(par + 1) * D]
                    if tt >= 2:
                        nc.scalar.wait_ge(k.ve.h, prev[tt - 2][1])
                    ncv.wait_ge(k.sA.h, xn_store[0])
                    tld = dB(xt, src[r0:r0 + 128, :])
                    ncv.wait_ge(k.sB.h, tld)
                    prev[tt] = ln_core(xt, r0, par, eps)
                    xn_store[0] = prev[tt][0]
                return k.sA.n

            def load_xnT(dst3, tok0, ntok, xn_ready_sA):
                for ds in range(16):
                    nc.scalar.wait_ge(k.sA.h, xn_ready_sA)
                    dB(dst3[:, ds, 0:ntok],
                       xnscr[tok0:tok0 + ntok, ds * 128:(ds + 1) * 128],
                       transpose=True)
                return k.sB.n

            def residual_evac_tile(pY, xsrc, xdst, r0, par, rstd_col,
                                   state, do_ln):
                xt = xf32[:, par * D:(par + 1) * D]
                t_x = dB(xt, xsrc[r0:r0 + 128, :])
                ncv.wait_ge(k.sB.h, t_x)
                ncv.wait_ge(k.sA.h, state.get("ev_store", 0))
                ncv.wait_ge(k.ve.h, k.ve.n)
                if rstd_col is None:
                    t_ev = k.ve.inc(ncv.tensor_tensor(evf32[:, :], pY, xt, ALU.add))
                else:
                    t_ev = k.ve.inc(ncv.scalar_tensor_tensor(
                        evf32[:, :], pY, rstd_col, xt, ALU.mult, ALU.add))
                sync.wait_ge(k.ve.h, t_ev)
                dA(xdst[r0:r0 + 128, :], evf32[:, :])
                state["ev_store"] = k.sA.n
                if do_ln:
                    ncv.wait_ge(k.sA.h, xn_store[0])
                    t_st, _ = ln_core(evf32[:, :], r0, par, EPS_LN)
                    xn_store[0] = t_st
                return t_ev

            def attn_a_all(xsrc, xdst, xn_ready):
                t_x0 = load_xnT(xnT, 0, LI, xn_ready)
                t_xnT = load_xnT(xnT1, LI, LI, xn_ready)
                nct.wait_ge(k.sB.h, t_xnT)

                wt_dma = {}
                mm_done = {}
                ev_done = {}
                row_t = {}
                nc.gpsimd.wait_ge(k.sW.h, t_wqkv_cast)
                gi = 0
                for ft in range(48):
                    par = ft % 2
                    wt = (wt0, wt1)[par]
                    if ft >= 2:
                        nc.gpsimd.wait_ge(k.te.h, mm_done[ft - 2])
                    wt_dma[ft] = dW(
                        wt.rearrange("p (ds e) -> p ds e", e=128),
                        wqkv_bf[:, ft * 128:(ft + 1) * 128].rearrange(
                            "(ds p) e -> p ds e", p=128))
                    nct.wait_ge(k.sW.h, wt_dma[ft])
                    pump(2)
                    for it in range(2):
                        xn_it = (xnT, xnT1)[it]
                        ev_it = (evst, evst1)[it]
                        for qc in range(4):
                            pb = ps[:, (gi % 2) * 512:(gi % 2) * 512 + 512]
                            if gi >= 2:
                                nct.wait_ge(k.ve.h, ev_done[gi - 2])
                            for ds in range(16):
                                ins = nct.matmul(pb, wt[:, ds * 128:(ds + 1) * 128],
                                                 xn_it[:, ds, qc * 512:(qc + 1) * 512],
                                                 start=(ds == 0), stop=(ds == 15))
                            t_mm = k.te.inc(ins)
                            if it == 1 and qc == 3:
                                mm_done[ft] = t_mm
                            ncv.wait_ge(k.te.h, t_mm)
                            if qc == 0 and (ft, it) != (0, 0) and (ft, it) != (0, 1):
                                pft = (ft, it)
                                prow = row_t.get((ft - 1, it), 0)
                                ncv.wait_ge(k.sA.h, prow)
                            sc2 = QK_SCALE if ft < 16 else 1.0
                            ev_done[gi] = k.ve.inc(ncv.tensor_scalar(
                                ev_it[:, qc * 512:(qc + 1) * 512], pb,
                                bcols[:, ft:ft + 1], sc2, ALU.add, ALU.mult))
                            gi += 1
                        sync.wait_ge(k.ve.h, ev_done[gi - 1])
                        row_t[(ft, it)] = dA(qkvT[it, ft * 128:(ft + 1) * 128, :],
                                             ev_it[:, :])
                t_qkvT = k.sA.n

                for it in range(2):
                    for ts in range(16):
                        pp_step(
                            (k.sA.h, t_qkvT),
                            lambda st, it=it, ts=ts: dB(
                                st[:, :], qkvT[it, 2 * D:3 * D,
                                               ts * 128:(ts + 1) * 128],
                                transpose=True),
                            lambda st, it=it, ts=ts: dA(
                                vU[it, ts * 128:(ts + 1) * 128, :], st[:, :]))
                t_vU = k.sA.n
                t_a2_entry = (k.te.n, k.ve.n)

                ncv.wait_ge(k.sA.h, t_vU)
                t_ones = k.ve.inc(ncv.memset(vp2[0][:, :, 128:132], 1.0))
                t_ones = k.ve.inc(ncv.memset(vp2[1][:, :, 128:132], 1.0))
                ncv.wait_ge(k.ve.h, t_ones)
                oh_store = 0
                head_done = {}
                prev_exp = None
                prev_pv = None
                for it in range(2):
                    for h in range(NH):
                        hh = it * NH + h
                        par = hh % 2
                        kth, qth, vp = kth2[par], qth2[par], vp2[par]
                        if hh >= 2:
                            sync.wait_ge(k.te.h, head_done[hh - 2][0])
                            sync.wait_ge(k.ve.h, head_done[hh - 2][1])
                        else:
                            sync.wait_ge(k.te.h, t_a2_entry[0])
                            sync.wait_ge(k.ve.h, t_a2_entry[1])
                        sync.wait_ge(k.sA.h, t_vU)
                        dA(kth[:, :], qkvT[it, D + h * 128:D + (h + 1) * 128, :])
                        dA(qth[:, :], qkvT[it, h * 128:(h + 1) * 128, :])
                        dA(vp[:, :, 0:128],
                           vU[it, :, h * 128:(h + 1) * 128].rearrange(
                               "(kt p) e -> p kt e", p=128))
                        t_ld = k.sA.n
                        pump_pp(8)
                        nct.wait_ge(k.sA.h, t_ld)
                        exp_done = {}
                        pv_ev = {}
                        pts_flat = big[:, 32768:40960]
                        for qc in range(4):
                            t_exp_last = 0
                            for g in range(4):
                                gg = qc * 4 + g
                                gpar = gg % 2
                                base = gpar * 2048
                                if gg >= 2:
                                    nct.wait_ge(k.se.h, exp_done[gg - 2])
                                elif prev_exp is not None:
                                    nct.wait_ge(k.se.h, prev_exp[14 + gg])
                                if gpar == 1 and qc > 0 and g == 1:
                                    nct.wait_ge(k.ve.h, pv_ev[(qc - 1) * 4 + 3])
                                elif gpar == 1 and qc == 0 and g == 1 and prev_pv is not None:
                                    nct.wait_ge(k.ve.h, prev_pv[15])
                                for kt4 in range(4):
                                    kt = g * 4 + kt4
                                    ins = nct.matmul(
                                        ps[:, base + kt4 * 512:base + (kt4 + 1) * 512],
                                        kth[:, kt * 128:(kt + 1) * 128],
                                        qth[:, qc * 512:(qc + 1) * 512],
                                        start=True, stop=True)
                                t_mm = k.te.inc(ins)
                                ncs.wait_ge(k.te.h, t_mm)
                                t_exp_last = k.se.inc(ncs.activation(
                                    pts_flat[:, g * 2048:(g + 1) * 2048],
                                    ps[:, base:base + 2048], AF.Exp, bias=f32(-SHIFT)))
                                exp_done[gg] = t_exp_last
                            nct.wait_ge(k.se.h, t_exp_last)
                            for qt in range(4):
                                gq = qc * 4 + qt
                                po = ps[:, 2048 + (gq % 2) * 512:2048 + (gq % 2) * 512 + 132]
                                if gq >= 2:
                                    nct.wait_ge(k.ve.h, pv_ev[gq - 2])
                                elif prev_pv is not None:
                                    nct.wait_ge(k.ve.h, prev_pv[14 + gq])
                                if gq == 0:
                                    ncv.wait_ge(k.sA.h, oh_store)
                                for kt in range(16):
                                    ins = nct.matmul(po[:, 0:129],
                                                     pts[:, kt, qt * 128:(qt + 1) * 128],
                                                     vp[:, kt, 0:129],
                                                     start=(kt == 0), stop=(kt == 15))
                                t_pv = k.te.inc(ins)
                                ncv.wait_ge(k.te.h, t_pv)
                                rc = recb[:, gq % 2:gq % 2 + 1]
                                k.ve.inc(ncv.reciprocal(rc, po[:, 128:129]))
                                ncv.wait_ge(k.ve.h, k.ve.n)
                                pv_ev[gq] = k.ve.inc(ncv.tensor_scalar(
                                    oh[:, gq, :], po[:, 0:128], rc, None, ALU.mult))
                        sync.wait_ge(k.ve.h, pv_ev[15])
                        oh_store = dA(
                            attn[it, :, h * 128:(h + 1) * 128].rearrange(
                                "(j p) e -> p j e", p=128),
                            oh[:, :, :])
                        head_done[hh] = (k.te.n, k.ve.n)
                        prev_exp = exp_done
                        prev_pv = pv_ev
                t_attn = k.sA.n

                nc.gpsimd.wait_ge(k.sA.h, t_attn)
                nc.gpsimd.wait_ge(k.sW.h, t_wouta_cast)
                for ds in range(0, 16, 4):
                    dW(wouta_sb[:, ds:ds + 4, :],
                       wouta_bf[ds * 128:(ds + 4) * 128, :].rearrange(
                           "(q p) e -> p q e", p=128))
                t_wo = k.sW.n
                state = {}
                aoT_t = 0
                for it in range(2):
                    tok0 = it * LI
                    nc.scalar.wait_ge(k.te.h, k.te.n)
                    for ds in range(16):
                        nc.scalar.wait_ge(k.sA.h, t_attn)
                        dB(aoT[:, ds, :], attn[it, :, ds * 128:(ds + 1) * 128],
                           transpose=True)
                    t_aoT = k.sB.n
                    nct.wait_ge(k.sW.h, t_wo)
                    nct.wait_ge(k.sB.h, t_aoT)
                    nct.wait_ge(k.ve.h, k.ve.n)
                    ev_t = {}
                    for tt in range(16):
                        pump_pp(2)
                        par = tt % 2
                        pY = ps[:, par * 2048:par * 2048 + 2048]
                        if tt >= 2:
                            nct.wait_ge(k.ve.h, ev_t[tt - 2])
                        for ds in range(16):
                            for c in range(4):
                                nct.matmul(pY[:, c * 512:(c + 1) * 512],
                                           aoT[:, ds, tt * 128:(tt + 1) * 128],
                                           wouta_sb[:, ds, c * 512:(c + 1) * 512],
                                           start=(ds == 0), stop=False)
                        for c in range(4):
                            ins = nct.matmul(pY[:, c * 512:(c + 1) * 512],
                                             ones_bf[0:1, :],
                                             brow_bf[0:1, c * 512:(c + 1) * 512],
                                             start=False, stop=True)
                        t_mm = k.te.inc(ins)
                        ncv.wait_ge(k.te.h, t_mm)
                        ev_t[tt] = residual_evac_tile(pY, xsrc, xdst,
                                                      tok0 + tt * 128,
                                                      par, None, state, do_ln=True)
                return k.sA.n

            def ffn(xsrc, xdst, xn_ready, emit_ln=False):
                chunks = []
                t0 = 0
                while t0 < NTOK:
                    cl = min(CH, NTOK - t0)
                    chunks.append((t0, cl))
                    t0 += cl
                t_gate_prev = 0
                dn_ev_last = 0
                ln_pending = None
                for ci, (c0, cl) in enumerate(chunks):
                    ntt = cl // 128
                    nc.scalar.wait_ge(k.te.h, t_gate_prev)
                    t_xnc = load_xnT(xnT_c, c0, cl, xn_ready)
                    nct.wait_ge(k.sB.h, t_xnc)
                    if ci > 0:
                        nct.wait_ge(k.ve.h, dn_ev_last)
                    ght = {}
                    for ht in range(64):
                        par = ht % 2
                        if ht >= 2:
                            nc.gpsimd.wait_ge(k.te.h, ght[ht - 2][0])
                        else:
                            nc.gpsimd.wait_ge(k.te.h, t_gate_prev)
                        t_w = dW(wgu[par], wguT[ht].rearrange("g ds p e -> p g ds e"))
                        pg = ps[:, par * 1024:par * 1024 + cl]
                        pu = ps[:, 2048 + par * 1024:2048 + par * 1024 + cl]
                        nct.wait_ge(k.sW.h, t_w)
                        if ht >= 2:
                            nct.wait_ge(k.ve.h, ght[ht - 2][1])
                        nsplit = [(0, 512), (512, cl)] if cl > 512 else [(0, cl)]
                        for ds in range(16):
                            for (a, b) in nsplit:
                                ins = nct.matmul(pg[:, a:b],
                                                 wgu[par][:, 0, ds, :],
                                                 xnT_c[:, ds, a:b],
                                                 start=(ds == 0),
                                                 stop=(ds == 15 and b == cl))
                        t_g = k.te.inc(ins)
                        for ds in range(16):
                            for (a, b) in nsplit:
                                ins = nct.matmul(pu[:, a:b],
                                                 wgu[par][:, 1, ds, :],
                                                 xnT_c[:, ds, a:b],
                                                 start=(ds == 0),
                                                 stop=(ds == 15 and b == cl))
                        t_u = k.te.inc(ins)
                        ncs.wait_ge(k.te.h, t_g)
                        if ht >= 1:
                            ncs.wait_ge(k.ve.h, ght[ht - 1][1])
                        t_si = k.se.inc(ncs.activation(silu_sb[:, 0:cl], pg, AF.Silu))
                        ncv.wait_ge(k.se.h, t_si)
                        ncv.wait_ge(k.te.h, t_u)
                        t_gt = k.ve.inc(ncv.tensor_tensor(gTf[:, ht, 0:cl],
                                                          silu_sb[:, 0:cl], pu,
                                                          ALU.mult))
                        ght[ht] = (t_u, t_gt)
                    t_gate_prev = ght[63][0]
                    nct.wait_ge(k.ve.h, ght[63][1])
                    nct.wait_ge(k.ve.h, ght[62][1])
                    dn_mm = {}
                    dn_ev = {}
                    slice_store = {}
                    slice_ev = {}
                    for qd in range(4):
                        for hsg in range(16):
                            par = hsg % 2
                            gidx = qd * 16 + hsg
                            if gidx >= 2:
                                nc.gpsimd.wait_ge(k.te.h, dn_mm[gidx - 2])
                            else:
                                nc.gpsimd.wait_ge(k.te.h, ght[63][0])
                            t_wd = dW(wdt[par],
                                      wdT[hsg * 512:(hsg + 1) * 512,
                                          qd * 512:(qd + 1) * 512].rearrange(
                                              "(j p) e -> p j e", p=128))
                            nct.wait_ge(k.sW.h, t_wd)
                            if hsg == 0 and qd > 0:
                                nct.wait_ge(k.ve.h, dn_ev[(qd - 1, ntt - 1)])
                            for j in range(4):
                                hs = hsg * 4 + j
                                for tt in range(ntt):
                                    ins = nct.matmul(ps[:, tt * 512:(tt + 1) * 512],
                                                     gTf[:, hs, tt * 128:(tt + 1) * 128],
                                                     wdt[par][:, j, :],
                                                     start=(hs == 0), stop=(hs == 63))
                            dn_mm[gidx] = k.te.inc(ins)
                        t_q = dn_mm[qd * 16 + 15]
                        for tt in range(ntt):
                            r0 = c0 + tt * 128
                            sl = (tt % 2, qd % 2)
                            xt = xf32[:, (tt % 2) * D + (qd % 2) * 512:
                                      (tt % 2) * D + (qd % 2) * 512 + 512]
                            if sl in slice_ev:
                                nc.scalar.wait_ge(k.ve.h, slice_ev[sl])
                            t_x = dB(xt, xsrc[r0:r0 + 128, qd * 512:(qd + 1) * 512])
                            ncv.wait_ge(k.sB.h, t_x)
                            ncv.wait_ge(k.te.h, t_q)
                            if (qd % 2,) in slice_store:
                                ncv.wait_ge(k.sA.h, slice_store[(qd % 2,)])
                            ev = evf32[:, (qd % 2) * 512:(qd % 2) * 512 + 512]
                            t_ev = k.ve.inc(ncv.tensor_tensor(
                                ev, ps[:, tt * 512:(tt + 1) * 512], xt, ALU.add))
                            dn_ev[(qd, tt)] = t_ev
                            slice_ev[sl] = t_ev
                            sync.wait_ge(k.ve.h, t_ev)
                            dA(xdst[r0:r0 + 128, qd * 512:(qd + 1) * 512], ev)
                            slice_store[(qd % 2,)] = k.sA.n
                    dn_ev_last = dn_ev[(3, ntt - 1)]
                return k.sA.n

            def attn_b(item, xsrc, xdst, xn_ready, stop=None):
                tok0 = item * LI
                t_xnT = load_xnT(xnT, tok0, LI, xn_ready)
                nct.wait_ge(k.sB.h, t_xnT)

                mm_done = {}
                ev_done = {}
                for ft in range(64):
                    par = ft % 2
                    wt = (wt0, wt1)[par]
                    if ft >= 2:
                        nc.gpsimd.wait_ge(k.te.h, mm_done[ft - 2])
                    t_w = dW(wt.rearrange("p (ds e) -> p ds e", e=128),
                             wqkvuT[ft].rearrange("ds p e -> p ds e"))
                    nct.wait_ge(k.sW.h, t_w)
                    for qc in range(4):
                        gi = ft * 4 + qc
                        pb = ps[:, (gi % 2) * 512:(gi % 2) * 512 + 512]
                        if gi >= 2:
                            nct.wait_ge(k.se.h, ev_done[gi - 2])
                        for ds in range(16):
                            ins = nct.matmul(pb, wt[:, ds * 128:(ds + 1) * 128],
                                             xnT[:, ds, qc * 512:(qc + 1) * 512],
                                             start=(ds == 0), stop=(ds == 15))
                        t_mm = k.te.inc(ins)
                        if qc == 3:
                            mm_done[ft] = t_mm
                        ncs.wait_ge(k.te.h, t_mm)
                        func = AF.Silu if ft < 32 else AF.Copy
                        ev_done[gi] = k.se.inc(ncs.activation(
                            evst[:, qc * 512:(qc + 1) * 512], pb, func))
                    sync.wait_ge(k.se.h, ev_done[ft * 4 + 3])
                    t_row = dA(qkvuTb[ft * 128:(ft + 1) * 128, :], evst[:, :])
                    ncs.wait_ge(k.sA.h, t_row)
                t_qkvuT = k.sA.n
                if stop == "b2a":
                    barrier()
                    return k.sA.n

                for j, (src_r, dst) in enumerate(((D, kUb), (2 * D, vUb))):
                    for ts in range(16):
                        pp_step(
                            (k.sA.h, t_qkvuT),
                            lambda st, src_r=src_r, ts=ts: dB(
                                st[:, :], qkvuTb[src_r:src_r + D,
                                                 ts * 128:(ts + 1) * 128],
                                transpose=True),
                            lambda st, dst=dst, ts=ts: dA(
                                dst[ts * 128:(ts + 1) * 128, :], st[:, :]))
                t_kvU = k.sA.n
                if stop == "b2a5":
                    barrier()
                    return k.sA.n

                psO = ps[:, 0:256]
                psQK = (ps[:, 512:768], ps[:, 768:1024])
                psRC = ps[:, 1024:1040]
                psKV = ps[:, 1536:1664]
                for h in range(SCAN_NH):
                    sync.wait_ge(k.te.h, k.te.n)
                    sync.wait_ge(k.ve.h, k.ve.n)
                    sync.wait_ge(k.sA.h, t_kvU)
                    dA(qtb[:, :], qkvuTb[h * 128:(h + 1) * 128, :])
                    dA(ktb[:, :], qkvuTb[D + h * 128:D + (h + 1) * 128, :])
                    dA(utb[:, :], qkvuTb[3 * D + h * 128:3 * D + (h + 1) * 128, :])
                    dA(ks3[:, :, :], kUb[:, h * 128:(h + 1) * 128].rearrange(
                        "(j p) e -> p j e", p=128))
                    dA(vs3[:, :, :], vUb[:, h * 128:(h + 1) * 128].rearrange(
                        "(j p) e -> p j e", p=128))
                    t_ld = k.sA.n
                    scol = slope_sb[:, h:h + 1]
                    ncv.wait_ge(k.te.h, k.te.n)
                    ncv.wait_ge(k.ve.h, k.ve.n)
                    t1 = k.ve.inc(ncv.tensor_scalar(dec_f[:, :], iota_sb[:, :],
                                                    scol, None, ALU.mult))
                    ncs.wait_ge(k.ve.h, t1)
                    ncs.wait_ge(k.se.h, k.se.n)
                    tq = k.se.inc(ncs.activation(qdec[:, :], dec_f[:, :], AF.Exp,
                                                 scale=f32(-1.0)))
                    for st in range(2):
                        ncv.wait_ge(k.se.h, k.se.n)
                        t2 = k.ve.inc(ncv.tensor_scalar(dec_f[:, :],
                                                        idxt_sb[:, st, :],
                                                        scol, None, ALU.mult))
                        ncs.wait_ge(k.ve.h, t2)
                        t3 = k.se.inc(ncs.activation(dtmp[:, :], dec_f[:, :],
                                                     AF.Exp, scale=f32(-1.0)))
                        ncv.wait_ge(k.se.h, t3)
                        k.ve.inc(ncv.tensor_tensor(ddm[:, st, :], dtmp[:, :],
                                                   maskt_sb[:, st, :], ALU.mult))
                    ncv.wait_ge(k.se.h, k.se.n)
                    t4 = k.ve.inc(ncv.tensor_scalar(dec_f[:, 0:2], iotc_sb[:, :],
                                                    256.0, scol,
                                                    ALU.subtract, ALU.mult))
                    ncs.wait_ge(k.ve.h, t4)
                    t5 = k.se.inc(ncs.activation(kdec[:, :], dec_f[:, 0:2], AF.Exp))
                    t5b = k.se.inc(ncs.activation(bdcol[:, :], scol, AF.Exp,
                                                  scale=f32(-256.0)))
                    ncv.wait_ge(k.se.h, t5b)
                    k.ve.inc(ncv.memset(kv_sb[:, :], 0.0))
                    t_kvbf = k.ve.inc(ncv.memset(kv_bf[:, :], 0.0))
                    nct.wait_ge(k.sA.h, t_ld)
                    if h == 0:
                        nct.wait_ge(k.se.h, k.se.n)
                    t_a = k.ve.n
                    t_sq = k.ve.n
                    t_gt = k.ve.n
                    t_cp_prev = k.se.n
                    for blk in range(SCAN_NB):
                        m0 = blk * BLK
                        nct.wait_ge(k.ve.h, t_a)
                        for st in range(2):
                            t_qk = k.te.inc(nct.matmul(
                                psQK[st], ktb[:, m0 + st * 128:m0 + st * 128 + 128],
                                qtb[:, m0:m0 + BLK], start=True, stop=True))
                        ncv.wait_ge(k.te.h, t_qk)
                        tqs = k.ve.inc(ncv.tensor_tensor(qsc[:, :],
                                                         qtb[:, m0:m0 + BLK],
                                                         qdec[:, :], ALU.mult))
                        for st in range(2):
                            t_a = k.ve.inc(ncv.tensor_tensor(asb[:, st, :],
                                                             psQK[st],
                                                             ddm[:, st, :],
                                                             ALU.mult))
                        for st in range(2):
                            t_kt = k.ve.inc(ncv.tensor_scalar(
                                ksc[:, st, :], ks3[:, 2 * blk + st, :],
                                kdec[:, st:st + 1], None, ALU.mult))
                        nct.wait_ge(k.ve.h, tqs)
                        nct.wait_ge(k.ve.h, t_kvbf)
                        nct.wait_ge(k.se.h, t_cp_prev)
                        nct.matmul(psO, kv_bf[:, :], qsc[:, :], start=True, stop=False)
                        nct.wait_ge(k.ve.h, t_a)
                        for st in range(2):
                            ins = nct.matmul(psO, vs3[:, 2 * blk + st, :],
                                             asb[:, st, :],
                                             start=False, stop=(st == 1))
                        t_o = k.te.inc(ins)
                        ncs.wait_ge(k.te.h, t_o)
                        ncs.wait_ge(k.ve.h, t_gt)
                        t_cp = k.se.inc(ncs.activation(o_sb[:, :], psO, AF.Copy))
                        ncv.wait_ge(k.se.h, t_cp)
                        t_gt = k.ve.inc(ncv.tensor_tensor(gTb[:, h, m0:m0 + BLK],
                                                          o_sb[:, :],
                                                          utb[:, m0:m0 + BLK],
                                                          ALU.mult))
                        t_sq = k.ve.inc(ncv.tensor_tensor(sq_bf[:, :], o_sb[:, :],
                                                          o_sb[:, :], ALU.mult))
                        nct.wait_ge(k.ve.h, t_sq)
                        for st in range(2):
                            c = 2 * blk + st
                            nct.matmul(psRC[:, c:c + 1],
                                       sq_bf[:, st * 128:st * 128 + 128],
                                       ones_col[:, 0:1],
                                       start=(h == 0 and blk == 0 and st == 0),
                                       stop=(h == SCAN_NH - 1 and blk == SCAN_NB - 1 and st == 1))
                        nct.wait_ge(k.ve.h, t_kt)
                        for st in range(2):
                            ins = nct.matmul(psKV, ksc[:, st, :],
                                             vs3[:, 2 * blk + st, :],
                                             start=(st == 0), stop=(st == 1))
                        t_kv = k.te.inc(ins)
                        ncv.wait_ge(k.te.h, t_kv)
                        k.ve.inc(ncv.scalar_tensor_tensor(kv_sb[:, :], kv_sb[:, :],
                                                          bdcol[:, 0:1], psKV,
                                                          ALU.mult, ALU.add))
                        ncv.wait_ge(k.ve.h, k.ve.n)
                        t_kvbf = k.ve.inc(ncv.tensor_copy(kv_bf[:, :], kv_sb[:, :]))
                        t_cp_prev = t_cp
                t_rc_done = k.te.n
                if stop == "scan":
                    barrier()
                    return k.sA.n

                ncs.wait_ge(k.te.h, t_rc_done)
                ncs.wait_ge(k.se.h, k.se.n)
                t_l = k.se.inc(ncs.activation(lnc[:, :], psRC, AF.Ln,
                                              bias=f32(EPS_RMS), scale=f32(1.0 / D)))
                ncs.wait_ge(k.se.h, t_l)
                t_rsc = k.se.inc(ncs.activation(rstdc[:, :], lnc[:, :], AF.Exp,
                                                scale=f32(-0.5)))

                nc.scalar.wait_ge(k.ve.h, k.ve.n)
                for hs in range(16):
                    dB(woutbT[:, hs, :], woutb_bf[:, hs * 128:(hs + 1) * 128],
                       transpose=True)
                t_wob = k.sB.n
                nct.wait_ge(k.sB.h, t_wob)
                nct.wait_ge(k.ve.h, k.ve.n)
                nct.wait_ge(k.se.h, t_rsc)
                ev_t = {}
                state = {}
                for tt in range(16):
                    par = tt % 2
                    pY = ps[:, par * 2048:par * 2048 + 2048]
                    if tt >= 2:
                        nct.wait_ge(k.ve.h, ev_t[tt - 2])
                    for hs in range(16):
                        for c in range(4):
                            ins = nct.matmul(pY[:, c * 512:(c + 1) * 512],
                                             gTb[:, hs, tt * 128:(tt + 1) * 128],
                                             woutbT[:, hs, c * 512:(c + 1) * 512],
                                             start=(hs == 0), stop=(hs == 15))
                    t_mm = k.te.inc(ins)
                    ncv.wait_ge(k.te.h, t_mm)
                    ncv.wait_ge(k.se.h, t_rsc)
                    ev_t[tt] = residual_evac_tile(pY, xsrc, xdst, tok0 + tt * 128,
                                                  par, rstdc[:, tt:tt + 1],
                                                  state, do_ln=True)
                return k.sA.n

            def stop(name):
                return stop_after == name

            while True:
                if stop("prepass"):
                    barrier()
                    break
                t_ln0 = ln_range(x_in, 0, NTI)
                t_ln1 = ln_range(x_in, LI, NTI)
                soft()
                if stop("ln0"):
                    barrier()
                    break
                prep_enqueue_ffn()
                attn_a_all(x_in, xs1, k.sA.n)
                while bgc:
                    bgc.popleft()()
                while bgp:
                    bgp.popleft()()
                t_prep = k.sA.n
                soft()
                if stop("attn_a1") or stop("attn_a0"):
                    barrier()
                    break
                nc.gpsimd.wait_ge(k.sA.h, t_prep)
                ffn(xs1, xs2, k.sA.n)
                t_prep2 = prep_qkvu()
                nc.gpsimd.wait_ge(k.sA.h, t_prep2)
                soft()
                if stop("ffn1"):
                    barrier()
                    break
                t_ln2 = ln_range(xs2, 0, NTT, x_ready=k.sA.n)
                soft()
                if stop("ln2"):
                    barrier()
                    break
                if stop_after in ("b2a", "b2a5", "scan"):
                    attn_b(0, xs2, xs3, t_ln2)
                    break
                attn_b(0, xs2, xs3, t_ln2)
                soft()
                if stop("attn_b0"):
                    barrier()
                    break
                attn_b(1, xs2, xs3, t_ln2)
                soft()
                if stop("attn_b1"):
                    barrier()
                    break
                ffn(xs3, y, k.sA.n)
                barrier()
                break

    ctx.close()
    return nc


def kernel(L0_attn_a_x, L0_attn_a_w_qkv, L0_attn_a_b_qkv, L0_attn_a_w_out,
           L0_attn_a_b_out, L0_attn_b_slope_rate, L0_attn_b_W_qkvu,
           L0_attn_b_W_out, L0_ffn_w_gate, L0_ffn_w_up, L0_ffn_w_down):
    x = np.ascontiguousarray(np.asarray(L0_attn_a_x, dtype=np.float32))
    wqkv = np.ascontiguousarray(np.asarray(L0_attn_a_w_qkv, dtype=np.float32))
    bqkv = np.ascontiguousarray(np.asarray(L0_attn_a_b_qkv, dtype=np.float32))
    wouta_ = np.ascontiguousarray(np.asarray(L0_attn_a_w_out, dtype=np.float32))
    bouta = np.ascontiguousarray(np.asarray(L0_attn_a_b_out, dtype=np.float32))
    slope_ = np.ascontiguousarray(np.asarray(L0_attn_b_slope_rate, dtype=np.float32))
    wqkvu_ = np.ascontiguousarray(np.asarray(L0_attn_b_W_qkvu, dtype=np.float32))
    woutb_ = np.ascontiguousarray(np.asarray(L0_attn_b_W_out, dtype=np.float32))
    wgate = np.ascontiguousarray(np.asarray(L0_ffn_w_gate, dtype=np.float32))
    wup = np.ascontiguousarray(np.asarray(L0_ffn_w_up, dtype=np.float32))
    wdown = np.ascontiguousarray(np.asarray(L0_ffn_w_down, dtype=np.float32))

    bqkv_cols = np.ascontiguousarray(bqkv.reshape(48, 128).T)
    bout_row = np.ascontiguousarray(bouta.reshape(1, D))
    slope_b = np.ascontiguousarray(np.broadcast_to(slope_.reshape(1, NH), (128, NH)))
    ar = np.arange(1, BLK + 1, dtype=np.float32)
    iota_row = np.ascontiguousarray(np.broadcast_to(ar[None, :], (128, BLK)))
    iota_col2 = np.ascontiguousarray(ar.reshape(2, 128).T)
    sm = ar[None, :] - ar[:, None]
    idxt = np.ascontiguousarray(np.maximum(sm, 0.0).astype(np.float32))
    maskt = np.ascontiguousarray((sm >= 0).astype(np.float32))

    nc = build()
    in_maps = []
    for c in range(8):
        xi = np.ascontiguousarray(x[2 * c:2 * c + 2].reshape(NTOK, D))
        in_maps.append(dict(
            x=xi, wqkv=wqkv, bqkv_cols=bqkv_cols,
            bout_row=bout_row, wouta=wouta_, wqkvu=wqkvu_, woutb=woutb_,
            slope=slope_b, wg=wgate, wu=wup, wd=wdown, iota_row=iota_row,
            iota_col2=iota_col2, idxt=idxt, maskt=maskt))
    res = run_bass_kernel_spmd(nc, in_maps, core_ids=list(range(8)))
    out = np.empty((16, LI, D), dtype=np.float32)
    for c in range(8):
        out[2 * c:2 * c + 2] = res.results[c]["y"].reshape(2, LI, D)
    return out


if __name__ == "__main__":
    import reference as R
    d = R.setup_inputs(0)
    inputs = {kk: np.asarray(v) for kk, v in d.items()}
    out = kernel(**inputs)
    ref = np.asarray(R.reference(**d))
    num = float(((out - ref) ** 2).mean())
    den = float((ref ** 2).mean())
    print("resvar:", num / den)
